# version 03
# baseline (speedup 1.0000x reference)
import numpy as np
import ml_dtypes
import concourse.bass as bass
import concourse.mybir as mybir
from concourse.bass_utils import run_bass_kernel_spmd
from contextlib import ExitStack

F32 = mybir.dt.float32
BF16 = mybir.dt.bfloat16
AF = mybir.ActivationFunctionType
ALU = mybir.AluOpType
AX = mybir.AxisListType

T = 1024
KD = 32
G = 8
NZ = 8
EPS = 1e-5


class Ctr:
    def __init__(self, sem, step=1):
        self.sem, self.n, self.step = sem, 0, step


class Planner:
    ENGS = ("sync", "gpsimd", "tensor", "vector", "scalar")

    def __init__(self):
        self.ops = {e: [] for e in self.ENGS}

    def emit(self, ename, fn, wait=(), inc=None):
        ws = tuple((c.sem, v) for (c, v) in wait if v > 0)
        isem = iamt = None
        if inc is not None:
            inc.n += inc.step
            isem, iamt = inc.sem, inc.step

        def run(eng, ws=ws, fn=fn, isem=isem, iamt=iamt):
            for sem, v in ws:
                eng.wait_ge(sem, v)
            inst = fn(eng)
            if isem is not None:
                inst.then_inc(isem, iamt)
        self.ops[ename].append(run)


def build_program(debug_taps=(), n_groups=G, do_ffn1=True, do_mamba=True,
                  do_ffn2=True):
    nc = bass.Bass("TRN2", target_bir_lowering=False, debug=False)
    dram = {}

    def din(name, shape, dt=F32):
        dram[name] = nc.dram_tensor(name, shape, dt, kind="ExternalInput").ap()

    def dint(name, shape, dt=F32):
        dram[name] = nc.dram_tensor(name, shape, dt).ap()

    din("xT", [4096, T])
    din("Wb_t", [32, 128, 32, 128])
    din("bbimg", [128, 32])
    din("Acat_img", [128, 4096])
    din("Bcat_img", [128, 4096])
    din("Wg9_img", [128, 288])
    din("bthr_col", [128, 1])
    din("Win_t", [G, 18, 128, 32, 128])
    din("Wdt_img", [128, 4096])
    din("Wout_t", [G, 8, 128, 8, 4, 128])
    din("convw_img", [128, 320])
    din("convb_img", [128, 80])
    din("dtb_col", [128, 1])
    din("Alog_col", [128, 1])
    din("D_col", [128, 1])
    din("normw_img", [128, 64])
    din("id32", [128, 128])
    din("triu", [128, 128])
    din("phalo", [128, 4])
    din("mcoef", [128, 4])
    din("ccoef", [128, 16])
    din("repmat", [8, 128])
    outT = nc.dram_tensor("outT", [4096, T], F32, kind="ExternalOutput").ap()
    tapshapes = {
        "xn": [128, KD * 1024], "dt": [128, T], "acum": [128, T],
        "hs0": [128, 8192], "b0": [128, 1024], "c0": [128, 1024],
        "y0": [128, 8192], "yn0": [128, 8192], "sh0": [128, 8192],
        "hw": [128, 1024], "z9": [128, 72], "dtT": [128, 1024],
        "TB": [128, 1024], "PB": [128, 1024], "WC": [128, 512],
        "ib0": [128, 1024],
    }
    taps = {}
    for tp in debug_taps:
        taps[tp] = nc.dram_tensor("tap_" + tp, tapshapes[tp], F32,
                                  kind="ExternalOutput").ap()

    dint("park1", [4096, T])
    dint("park2", [4096, T])
    dint("gatepark", [G * 1024, T], BF16)
    dint("pubLP", [128, 1])
    dint("gathLP", [512, 1])
    for g in range(G):
        dint(f"pubH{g}", [128, 32], BF16)
        dint(f"gatH{g}", [512, 32], BF16)
        dint(f"pubF{g}", [128, 1024], BF16)
        dint(f"gatF{g}", [512, 1024], BF16)

    ctx = ExitStack()
    sbuf = lambda nm, sh, dt=F32: ctx.enter_context(nc.sbuf_tensor(nm, sh, dt))
    psum = lambda nm, sh, dt=F32: ctx.enter_context(nc.psum_tensor(nm, sh, dt))

    arena = sbuf("arena", [128, 53248], BF16)

    def asl(lo, hi):
        return arena[:, lo:hi]
    XN0 = 0
    HS0 = 32768
    SH0 = 40960
    BC0 = 49152

    wWA = sbuf("wWA", [128, 4096], BF16)
    wWB = sbuf("wWB", [128, 4096], BF16)
    Wg9 = sbuf("Wg9", [128, 288], BF16)
    bbimg = sbuf("bbimg_s", [128, 32])
    halfb = sbuf("halfb", [128, 32])
    convw = sbuf("convw_s", [128, 320])
    convb = sbuf("convb_s", [128, 80])
    dtbc = sbuf("dtbc", [128, 1])
    alogc = sbuf("alogc", [128, 1])
    acolc = sbuf("acolc", [128, 1])
    dcolc = sbuf("dcolc", [128, 1])
    nrmw = sbuf("nrmw", [128, 64])
    id32s = sbuf("id32s", [128, 128])
    idbfs = sbuf("idbfs", [128, 128], BF16)
    trius = sbuf("trius", [128, 128])
    phal = sbuf("phal", [128, 4])
    mco = sbuf("mco", [128, 4])
    cco = sbuf("cco", [128, 16])
    bthr = sbuf("bthr", [128, 1])
    repm = sbuf("repm", [8, 128])
    ones_f = sbuf("ones_f", [1, 128])
    neg128 = sbuf("neg128", [128, 128])
    onescbf = sbuf("onescbf", [128, 1], BF16)
    ones128 = sbuf("ones128", [128, 128])

    xin32 = sbuf("xin32", [128, 2048])
    castbf = sbuf("castbf", [128, 2048], BF16)
    sqbf = sbuf("sqbf", [128, 1024], BF16)
    srow4 = sbuf("srow4", [128, 1024])
    bc_mr = sbuf("bc_mr", [128, 1024])
    bc_r = sbuf("bc_r", [128, 1024])

    dts = sbuf("dts", [128, T])
    acum = sbuf("acum", [128, T])
    chsum = sbuf("chsum", [128, 8])
    tzfm = sbuf("tzfm", [128, 8])
    pzfm = sbuf("pzfm", [128, 9])
    dtT = sbuf("dtT", [128, 1024])
    decT = sbuf("decT", [128, 1024])
    TB = sbuf("TB", [128, 1024])
    PB = sbuf("PB", [128, 1024])
    DB = sbuf("DB", [128, 128])
    WC = sbuf("WC", [128, 512])
    lps = sbuf("lps", [128, 4])
    wlog = sbuf("wlog", [128, 4])
    lpcol = sbuf("lpcol", [128, 1])
    paf = sbuf("paf", [128, 1024])

    cva = sbuf("cva", [128, 2048])
    halog = sbuf("halog", [128, 128], BF16)
    haloc = sbuf("haloc", [128, 32], BF16)
    phbuf = sbuf("phbuf", [128, 32], BF16)
    Bfm = sbuf("Bfm", [128, 1024], BF16)
    Cfm = sbuf("Cfm", [128, 1024], BF16)
    Btm = sbuf("Btm", [128, 1024], BF16)
    xtmb = sbuf("xtmb", [128, 2048], BF16)
    bcmb = sbuf("bcmb", [128, 256])
    e1s = sbuf("e1s", [128, 256])
    msb = sbuf("msb", [128, 512], BF16)
    xds = sbuf("xds", [128, 256], BF16)
    ces = sbuf("ces", [128, 512], BF16)
    z9s = sbuf("z9s", [128, 72])
    wtk = sbuf("wtk", [128, 32])
    tapb = xin32

    mmA = psum("mmA", [128, 512])
    mmB = psum("mmB", [128, 512])
    sm32 = psum("sm32", [128, 512])
    psT = psum("psT", [128, 1024], BF16)
    psY = psum("psY", [128, 512])
    psS = psum("psS", [128, 512])
    st0 = psum("st0", [128, 512])
    st1 = psum("st1", [128, 512])

    def ctr(name, step=1):
        return Ctr(ctx.enter_context(nc.semaphore(name)), step)

    qS = ctr("qS", 16)
    qG = ctr("qG", 16)
    cPE = ctr("cPE")
    cDV = ctr("cDV")
    cAC = ctr("cAC")
    cPL = ctr("cPL")
    cCC = ctr("cCC")
    ALLC = (qS, qG, cPE, cDV, cAC, cPL, cCC)

    pl = Planner()
    EM = pl.emit

    def barrier():
        vals = tuple((c, c.n) for c in ALLC if c.n > 0)
        for e in Planner.ENGS:
            def run(eng, vals=vals):
                for c, v in vals:
                    eng.wait_ge(c.sem, v)
            pl.ops[e].append(run)

    def tap_dump(name, src, cols):
        if name not in taps:
            return
        barrier()
        for i in range((cols + 1023) // 1024):
            w = min(1024, cols - i * 1024)
            EM("vector", lambda e, i=i, w=w, src=src: e.tensor_copy(
                out=tapb[:, 0:w], in_=src[:, i * 1024:i * 1024 + w]), inc=cDV)
            EM("sync", lambda e, i=i, w=w, name=name: e.dma_start(
                out=taps[name][:, i * 1024:i * 1024 + w], in_=tapb[:, 0:w]),
               wait=[(cDV, cDV.n)], inc=qS)
            barrier()

    for sbt, src in [(Wg9, "Wg9_img"), (idbfs, "id32")]:
        EM("gpsimd", lambda e, d=sbt, s=src: e.dma_start(out=d[:, :], in_=dram[s]),
           inc=qG)
    for sbt, src in [(convw, "convw_img"), (convb, "convb_img"), (dtbc, "dtb_col"),
                     (alogc, "Alog_col"), (dcolc, "D_col"), (nrmw, "normw_img"),
                     (id32s, "id32"), (trius, "triu"), (mco, "mcoef"),
                     (cco, "ccoef"), (bthr, "bthr_col"), (repm, "repmat"),
                     (phal, "phalo"), (bbimg, "bbimg")]:
        EM("sync", lambda e, d=sbt, s=src: e.dma_start(out=d[:, :], in_=dram[s]),
           inc=qS)
    pre_q = [(qS, qS.n), (qG, qG.n)]
    for f in [
              lambda e: e.memset(ones_f[:, :], 1.0),
              lambda e: e.memset(neg128[:, :], -1.0),
              lambda e: e.memset(onescbf[:, :], 1.0),
              lambda e: e.memset(ones128[:, :], 1.0),
              lambda e: e.memset(pzfm[:, 0:1], 0.0),
              lambda e: e.memset(sqbf[:, :], 0.0)]:
        EM("vector", f)
    EM("vector", lambda e: e.memset(haloc[:, :], 0.0), wait=pre_q, inc=cDV)
    EM("vector", lambda e: e.tensor_scalar_mul(halfb[:, :], bbimg[:, :], 0.5),
       wait=pre_q, inc=cDV)
    EM("scalar", lambda e: e.activation(out=acolc[:, :], in_=alogc[:, :],
                                        func=AF.Exp), wait=pre_q, inc=cAC)
    EM("vector", lambda e: e.tensor_scalar_mul(acolc[:, :], acolc[:, :], -1.0),
       wait=[(cAC, cAC.n)], inc=cDV)
    EM("tensor", lambda e: e.transpose(sm32[0:1, 0:128], dcolc[:, :], id32s[:, :]),
       wait=[(qS, qS.n), (cDV, cDV.n)], inc=cPE)
    EM("vector", lambda e: e.tensor_copy(out=srow4[0:1, 0:128],
                                         in_=sm32[0:1, 0:128]),
       wait=[(cPE, cPE.n)], inc=cDV)
    EM("tensor", lambda e: e.matmul(sm32[:, 128:256], ones_f[0:1, 0:128],
                                    srow4[0:1, 0:128], start=True, stop=True),
       wait=[(cDV, cDV.n)], inc=cPE)
    EM("vector", lambda e: e.tensor_copy(out=DB[:, :], in_=sm32[:, 128:256]),
       wait=[(cPE, cPE.n)], inc=cDV)
    barrier()

    def ln_phase(src_name):
        src = dram[src_name]
        loc_cast, loc_sq, loc_mm = {}, {}, {}
        for kc in range(KD):
            sl = kc % 2
            w = [(cPE, loc_mm.get(kc - 2, 0)), (cAC, loc_sq.get(kc - 2, 0))]
            EM("sync", lambda e, kc=kc, sl=sl: e.dma_start(
                out=xin32[:, sl * 1024:sl * 1024 + 1024],
                in_=src[kc * 128:kc * 128 + 128, :]), wait=w, inc=qS)
            ldv = qS.n
            EM("vector", lambda e, sl=sl: e.tensor_copy(
                out=castbf[:, sl * 1024:sl * 1024 + 1024],
                in_=xin32[:, sl * 1024:sl * 1024 + 1024]),
               wait=[(qS, ldv)], inc=cDV)
            loc_cast[kc] = cDV.n
            EM("scalar", lambda e, sl=sl: e.activation(
                out=sqbf[:, 0:1024],
                in_=xin32[:, sl * 1024:sl * 1024 + 1024], func=AF.Square),
               wait=[(qS, ldv), (cPE, loc_mm.get(kc - 1, 0))], inc=cAC)
            loc_sq[kc] = cAC.n
            for tt in range(2):
                stp = st0 if tt == 0 else st1
                EM("tensor", lambda e, kc=kc, sl=sl, tt=tt, stp=stp: e.matmul(
                    stp[0:1, 0:512], onescbf[:, 0:1],
                    castbf[:, sl * 1024 + tt * 512:sl * 1024 + tt * 512 + 512],
                    start=(kc == 0), stop=(kc == KD - 1)),
                   wait=[(cDV, loc_cast[kc])], inc=cPE)
                EM("tensor", lambda e, kc=kc, sl=sl, tt=tt, stp=stp: e.matmul(
                    stp[64:65, 0:512], onescbf[:, 0:1],
                    sqbf[:, tt * 512:tt * 512 + 512],
                    start=(kc == 0), stop=(kc == KD - 1)),
                   wait=[(cAC, loc_sq[kc])], inc=cPE)
            loc_mm[kc] = cPE.n
        pe_done = cPE.n
        for tt in range(2):
            stp = st0 if tt == 0 else st1
            EM("vector", lambda e, tt=tt, stp=stp: e.tensor_copy(
                out=srow4[0:1, tt * 512:tt * 512 + 512], in_=stp[0:1, 0:512]),
               wait=[(cPE, pe_done)], inc=cDV)
            EM("vector", lambda e, tt=tt, stp=stp: e.tensor_copy(
                out=srow4[64:65, tt * 512:tt * 512 + 512], in_=stp[64:65, 0:512]),
               wait=[(cPE, pe_done)], inc=cDV)
        rows_d = cDV.n
        for tt in range(2):
            stp = st0 if tt == 0 else st1
            EM("tensor", lambda e, tt=tt, stp=stp: e.matmul(
                stp[:, 0:512], ones_f[0:1, 0:128],
                srow4[0:1, tt * 512:tt * 512 + 512], start=True, stop=True),
               wait=[(cDV, rows_d)], inc=cPE)
            EM("vector", lambda e, tt=tt, stp=stp: e.tensor_scalar_mul(
                bc_mr[:, tt * 512:tt * 512 + 512], stp[:, 0:512], 1.0 / 4096),
               wait=[(cPE, cPE.n)], inc=cDV)
            EM("tensor", lambda e, tt=tt, stp=stp: e.matmul(
                stp[:, 0:512], ones128[64:65, 0:128],
                srow4[64:65, tt * 512:tt * 512 + 512], start=True, stop=True),
               wait=[(cDV, cDV.n)], inc=cPE)
            EM("vector", lambda e, tt=tt, stp=stp: e.tensor_scalar_mul(
                bc_r[:, tt * 512:tt * 512 + 512], stp[:, 0:512], 1.0 / 4096),
               wait=[(cPE, cPE.n)], inc=cDV)
        bc0 = cDV.n
        EM("vector", lambda e: e.tensor_tensor(
            out=xin32[:, 0:1024], in0=bc_mr[:, :], in1=bc_mr[:, :],
            op=ALU.mult), wait=[(cDV, bc0)], inc=cDV)
        EM("vector", lambda e: e.tensor_tensor(
            out=bc_r[:, :], in0=bc_r[:, :], in1=xin32[:, 0:1024],
            op=ALU.subtract), inc=cDV)
        EM("vector", lambda e: e.tensor_scalar_add(bc_r[:, :], bc_r[:, :], EPS),
           inc=cDV)
        var_d = cDV.n
        EM("scalar", lambda e: e.activation(out=bc_r[:, :], in_=bc_r[:, :],
                                            func=AF.Sqrt),
           wait=[(cDV, var_d)], inc=cAC)
        EM("vector", lambda e: e.reciprocal(bc_r[:, :], bc_r[:, :]),
           wait=[(cAC, cAC.n)], inc=cDV)
        EM("vector", lambda e: e.tensor_tensor(
            out=bc_mr[:, :], in0=bc_mr[:, :], in1=bc_r[:, :], op=ALU.mult),
           inc=cDV)
        bcd = cDV.n
        nrm = {}
        for kc in range(KD):
            sl = kc % 2
            EM("sync", lambda e, kc=kc, sl=sl: e.dma_start(
                out=xin32[:, sl * 1024:sl * 1024 + 1024],
                in_=src[kc * 128:kc * 128 + 128, :]),
               wait=[(cPL, nrm.get(kc - 2, 0)), (cDV, bcd)], inc=qS)
            ldv = qS.n
            EM("vector", lambda e, sl=sl: e.tensor_tensor(
                out=xin32[:, sl * 1024:sl * 1024 + 1024],
                in0=xin32[:, sl * 1024:sl * 1024 + 1024],
                in1=bc_r[:, :], op=ALU.mult), wait=[(qS, ldv), (cDV, bcd)],
               inc=cDV)
            EM("gpsimd", lambda e, kc=kc, sl=sl: e.tensor_tensor(
                out=arena[:, XN0 + kc * 1024:XN0 + kc * 1024 + 1024],
                in0=xin32[:, sl * 1024:sl * 1024 + 1024],
                in1=bc_mr[:, :], op=ALU.subtract), wait=[(cDV, cDV.n)], inc=cPL)
            nrm[kc] = cPL.n
        barrier()

    def ffn_phase(which, parts='all'):
        resid_src = dram["xT"] if which == 1 else dram["park2"]
        dest = dram["park1"] if which == 1 else outT
        EM("gpsimd", lambda e: e.dma_start(out=asl(BC0, BC0 + 4096),
                                           in_=dram["Bcat_img"]), inc=qG)
        EM("gpsimd", lambda e: e.dma_start(out=wWB[:, :], in_=dram["Acat_img"]),
           inc=qG)
        acat_ld = qG.n
        for j in range(8):
            for kc in range(KD):
                EM("tensor", lambda e, j=j, kc=kc: e.matmul(
                    sm32[:, j * 9:j * 9 + 9],
                    arena[:, XN0 + kc * 1024 + j * 128:XN0 + kc * 1024 + j * 128 + 128],
                    Wg9[:, kc * 9:kc * 9 + 9],
                    start=(kc == 0), stop=(kc == KD - 1)), inc=cPE)
        pe_g9 = cPE.n
        EM("vector", lambda e: e.tensor_copy(out=z9s[:, :], in_=sm32[:, 0:72]),
           wait=[(cPE, pe_g9)], inc=cDV)
        z9d = cDV.n
        if parts == 'g9':
            tap_dump("z9", z9s, 72)
            EM("sync", lambda e: e.dma_start(out=dram["park1"], in_=dram["xT"]),
               inc=qS)
            barrier()
            return
        EM("vector", lambda e: e.tensor_scalar(
            wtk[:, 0:8],
            z9s[:, :].rearrange("p (j e) -> p j e", e=9)[:, :, 8],
            bthr[:, 0:1], None, ALU.add), wait=[(cDV, z9d)], inc=cDV)
        EM("scalar", lambda e: e.activation(out=wtk[:, 8:16], in_=wtk[:, 0:8],
                                            func=AF.Sigmoid, scale=1.0),
           wait=[(cDV, cDV.n)], inc=cAC)
        EM("vector", lambda e: e.tensor_scalar_mul(wtk[:, 8:16], wtk[:, 8:16],
                                                   0.125),
           wait=[(cAC, cAC.n)], inc=cDV)
        thr_d = cDV.n
        for j in range(8):
            EM("vector", lambda e, j=j: e.tensor_scalar_min(
                z9s[:, j * 9:j * 9 + 8], z9s[:, j * 9:j * 9 + 8], 30.0),
               wait=[(cDV, thr_d)], inc=cDV)
            EM("scalar", lambda e, j=j: e.activation(
                out=z9s[:, j * 9:j * 9 + 8], in_=z9s[:, j * 9:j * 9 + 8],
                func=AF.Exp), wait=[(cDV, cDV.n)], inc=cAC)
            EM("vector", lambda e, j=j: e.reduce_sum(
                wtk[:, 16 + j:17 + j], z9s[:, j * 9:j * 9 + 8], axis=AX.X),
               wait=[(cAC, cAC.n)], inc=cDV)
            EM("vector", lambda e, j=j: e.reciprocal(
                wtk[:, 24 + j:25 + j], wtk[:, 16 + j:17 + j]),
               wait=[(cDV, cDV.n)], inc=cDV)
            EM("vector", lambda e, j=j: e.tensor_scalar(
                z9s[:, j * 9:j * 9 + 8], z9s[:, j * 9:j * 9 + 8],
                wtk[:, 24 + j:25 + j], None, ALU.mult),
               wait=[(cDV, cDV.n)], inc=cDV)
            EM("vector", lambda e, j=j: e.tensor_scalar(
                z9s[:, j * 9:j * 9 + 8], z9s[:, j * 9:j * 9 + 8],
                wtk[:, 8 + j:9 + j], None, ALU.subtract),
               wait=[(cDV, cDV.n)], inc=cDV)
            EM("vector", lambda e, j=j: e.tensor_relu(
                z9s[:, j * 9:j * 9 + 8], z9s[:, j * 9:j * 9 + 8]),
               wait=[(cDV, cDV.n)], inc=cDV)
            EM("vector", lambda e, j=j: e.reduce_sum(
                wtk[:, 16 + j:17 + j], z9s[:, j * 9:j * 9 + 8], axis=AX.X),
               wait=[(cDV, cDV.n)], inc=cDV)
            EM("vector", lambda e, j=j: e.tensor_scalar_max(
                wtk[:, 16 + j:17 + j], wtk[:, 16 + j:17 + j], 1e-30),
               wait=[(cDV, cDV.n)], inc=cDV)
            EM("vector", lambda e, j=j: e.reciprocal(
                wtk[:, 24 + j:25 + j], wtk[:, 16 + j:17 + j]),
               wait=[(cDV, cDV.n)], inc=cDV)
            EM("vector", lambda e, j=j: e.tensor_scalar(
                z9s[:, j * 9:j * 9 + 8], z9s[:, j * 9:j * 9 + 8],
                wtk[:, 24 + j:25 + j], 2.0, ALU.mult, ALU.mult),
               wait=[(cDV, cDV.n)], inc=cDV)
        wd = cDV.n
        if which == 1:
            tap_dump("z9", z9s, 72)
        if parts == 'w':
            EM("sync", lambda e: e.dma_start(out=dram["park1"], in_=dram["xT"]),
               inc=qS)
            barrier()
            return
        for j in range(8):
            bk = sm32 if j < 4 else st0
            EM("tensor", lambda e, j=j, bk=bk: e.transpose(
                bk[0:8, (j % 4) * 128:(j % 4) * 128 + 128],
                z9s[:, j * 9:j * 9 + 8], id32s[:, :]),
               wait=[(cDV, wd)], inc=cPE)
        wtr = cPE.n
        for j in range(8):
            bk = sm32 if j < 4 else st0
            EM("vector", lambda e, j=j, bk=bk: e.tensor_copy(
                out=bc_r[0:8, j * 128:j * 128 + 128],
                in_=bk[0:8, (j % 4) * 128:(j % 4) * 128 + 128]),
               wait=[(cPE, wtr)], inc=cDV)
        wrd = cDV.n
        if parts == 'wt':
            EM("sync", lambda e: e.dma_start(out=dram["park1"], in_=dram["xT"]),
               inc=qS)
            barrier()
            return
        for tt in range(2):
            stp = mmA if tt == 0 else mmB
            for kc in range(KD):
                EM("tensor", lambda e, tt=tt, kc=kc, stp=stp: e.matmul(
                    stp[:, 0:512], wWB[:, kc * 128:kc * 128 + 128],
                    arena[:, XN0 + kc * 1024 + tt * 512:XN0 + kc * 1024 + tt * 512 + 512],
                    start=(kc == 0), stop=(kc == KD - 1)),
                   wait=([(qG, acat_ld), (cDV, wrd)] if kc == 0 else []), inc=cPE)
            hmm = cPE.n
            if parts == 'h1':
                continue
            EM("tensor", lambda e, tt=tt: e.matmul(
                sm32[:, 0:512], repm[0:8, :],
                bc_r[0:8, tt * 512:tt * 512 + 512], start=True, stop=True),
               wait=[(cDV, wrd), (cPE, hmm), (cDV, cDV.n)], inc=cPE)
            wbd = cPE.n
            EM("vector", lambda e, tt=tt, stp=stp: e.tensor_copy(
                out=Bfm[:, tt * 512:tt * 512 + 512], in_=stp[:, 0:512]),
               wait=[(cPE, wbd)], inc=cDV)
            EM("vector", lambda e, tt=tt: e.tensor_tensor(
                out=Bfm[:, tt * 512:tt * 512 + 512],
                in0=Bfm[:, tt * 512:tt * 512 + 512],
                in1=sm32[:, 0:512], op=ALU.mult), wait=[(cDV, cDV.n)], inc=cDV)
        hwd = cDV.n
        if parts == 'h1':
            EM("sync", lambda e: e.dma_start(out=dram["park1"], in_=dram["xT"]),
               inc=qS)
            barrier()
            return
        if which == 1:
            tap_dump("hw", Bfm, 1024)
        if parts == 'h':
            EM("sync", lambda e: e.dma_start(out=dram["park1"], in_=dram["xT"]),
               inc=qS)
            barrier()
            return
        if parts == 'smalls':
            EM("sync", lambda e: e.dma_start(out=dram["park1"], in_=dram["xT"]),
               inc=qS)
            barrier()
            return
        slab_mm = {}
        ep_st = {}
        for oc in range(KD):
            wb = wWA
            wb = wWA if oc % 2 == 0 else wWB
            wwait = [(cPE, slab_mm.get(oc - 2, 0))]
            if oc % 2 == 1 or oc == 0:
                pass
            if oc <= 1:
                wwait.append((cDV, hwd))
            EM("gpsimd", lambda e, oc=oc, wb=wb: e.dma_start(
                out=wb[:, :],
                in_=dram["Wb_t"][oc, :, :, :].rearrange("d k o -> d (k o)")),
               wait=wwait + ([(cPE, cPE.n)] if oc == 1 else []), inc=qG)
            wld = qG.n
            for tt in range(2):
                pp = [mmA, mmB, psY, psS][(oc % 2) * 2 + tt]
                for kc in range(KD):
                    EM("tensor", lambda e, oc=oc, tt=tt, kc=kc, pp=pp, wb=wb:
                       e.matmul(pp[:, 0:512], wb[:, kc * 128:kc * 128 + 128],
                                arena[:, XN0 + kc * 1024 + tt * 512:XN0 + kc * 1024 + tt * 512 + 512],
                                start=(kc == 0), stop=False),
                       wait=([(qG, wld), (cDV, ep_st.get((oc - 2, tt), hwd))]
                             if kc == 0 else []), inc=cPE)
                EM("tensor", lambda e, oc=oc, tt=tt, pp=pp: e.matmul(
                    pp[:, 0:512], asl(BC0, BC0 + 4096)[:, oc * 128:oc * 128 + 128],
                    Bfm[:, tt * 512:tt * 512 + 512], start=False, stop=True),
                   wait=[(cDV, hwd)] if oc == 0 else [], inc=cPE)
                mmdone = cPE.n
                slab_mm[oc] = mmdone
                sl = (oc * 2 + tt) % 4
                EM("sync", lambda e, oc=oc, tt=tt, sl=sl: e.dma_start(
                    out=xin32[:, sl * 512:sl * 512 + 512],
                    in_=resid_src[oc * 128:oc * 128 + 128, tt * 512:tt * 512 + 512]),
                   wait=[(cDV, ep_st.get((oc - 2, tt), 0))], inc=qS)
                rld = qS.n
                EM("vector", lambda e, oc=oc, tt=tt, pp=pp, sl=sl: e.tensor_scalar(
                    cva[:, tt * 1024:tt * 1024 + 512], pp[:, 0:512], 0.5,
                    halfb[:, oc:oc + 1], ALU.mult, ALU.add),
                   wait=[(cPE, mmdone)], inc=cDV)
                EM("vector", lambda e, oc=oc, tt=tt, sl=sl: e.tensor_tensor(
                    out=cva[:, tt * 1024:tt * 1024 + 512],
                    in0=cva[:, tt * 1024:tt * 1024 + 512],
                    in1=xin32[:, sl * 512:sl * 512 + 512], op=ALU.add),
                   wait=[(qS, rld)], inc=cDV)
                evd = cDV.n
                ep_st[(oc, tt)] = evd
                EM("sync", lambda e, oc=oc, tt=tt, dest=dest: e.dma_start(
                    out=dest[oc * 128:oc * 128 + 128, tt * 512:tt * 512 + 512],
                    in_=cva[:, tt * 1024:tt * 1024 + 512]),
                   wait=[(cDV, evd)], inc=qS)
        barrier()

    def dt_phase():
        EM("gpsimd", lambda e: e.dma_start(out=wWA[:, :], in_=dram["Wdt_img"]),
           inc=qG)
        wld = qG.n
        for tt in range(2):
            pp = mmA if tt == 0 else mmB
            for kc in range(KD):
                EM("tensor", lambda e, tt=tt, kc=kc, pp=pp: e.matmul(
                    pp[:, 0:512], wWA[:, kc * 128:kc * 128 + 128],
                    arena[:, XN0 + kc * 1024 + tt * 512:XN0 + kc * 1024 + tt * 512 + 512],
                    start=(kc == 0), stop=(kc == KD - 1)),
                   wait=[(qG, wld)] if kc == 0 else [], inc=cPE)
            EM("vector", lambda e, tt=tt, pp=pp: e.tensor_copy(
                out=bc_mr[:, tt * 512:tt * 512 + 512], in_=pp[:, 0:512]),
               wait=[(cPE, cPE.n)], inc=cDV)
        EM("vector", lambda e: e.tensor_scalar(
            bc_mr[:, :], bc_mr[:, :], dtbc[:, 0:1], None, ALU.add),
           wait=[(cDV, cDV.n)], inc=cDV)
        a_d = cDV.n
        EM("scalar", lambda e: e.activation(out=dts[:, :], in_=bc_mr[:, :],
                                            func=AF.Abs),
           wait=[(cDV, a_d)], inc=cAC)
        EM("scalar", lambda e: e.activation(out=dts[:, :], in_=dts[:, :],
                                            func=AF.Exp, scale=-1.0), inc=cAC)
        EM("vector", lambda e: e.tensor_scalar_add(dts[:, :], dts[:, :], 1.0),
           wait=[(cAC, cAC.n)], inc=cDV)
        EM("scalar", lambda e: e.activation(out=dts[:, :], in_=dts[:, :],
                                            func=AF.Ln),
           wait=[(cDV, cDV.n)], inc=cAC)
        EM("vector", lambda e: e.tensor_scalar_max(bc_mr[:, :], bc_mr[:, :], 0.0),
           wait=[(cAC, cAC.n)], inc=cDV)
        EM("vector", lambda e: e.tensor_tensor(
            out=dts[:, :], in0=dts[:, :], in1=bc_mr[:, :], op=ALU.add),
           wait=[(cDV, cDV.n)], inc=cDV)
        EM("vector", lambda e: e.tensor_scalar_min(dts[:, :], dts[:, :], 100.0),
           wait=[(cDV, cDV.n)], inc=cDV)
        dt_d = cDV.n
        tap_dump("dt", dts, T)
        EM("vector", lambda e: e.tensor_scalar(
            acum[:, :], dts[:, :], acolc[:, 0:1], None, ALU.mult),
           wait=[(cDV, dt_d)], inc=cDV)
        ad_d = cDV.n
        for z in range(NZ):
            EM("vector", lambda e, z=z: e.tensor_tensor_scan(
                acum[:, z * 128:z * 128 + 128], ones128[:, :],
                acum[:, z * 128:z * 128 + 128], 0.0, ALU.mult, ALU.add),
               wait=[(cDV, ad_d)], inc=cDV)
        ac_d = cDV.n
        tap_dump("acum", acum, T)
        for z in range(NZ):
            EM("vector", lambda e, z=z: e.tensor_scalar(
                bc_mr[:, z * 128:z * 128 + 128], acum[:, z * 128:z * 128 + 128],
                acum[:, z * 128 + 127:z * 128 + 128], None, ALU.subtract),
               wait=[(cDV, ac_d)], inc=cDV)
            EM("scalar", lambda e, z=z: e.activation(
                out=bc_mr[:, z * 128:z * 128 + 128],
                in_=bc_mr[:, z * 128:z * 128 + 128], func=AF.Exp, scale=-1.0),
               wait=[(cDV, cDV.n)], inc=cAC)
        dec_d = cAC.n
        EM("vector", lambda e: e.tensor_copy(
            out=chsum[:, :],
            in_=acum[:, :].rearrange("p (z l) -> p z l", l=128)[:, :, 127]),
           wait=[(cDV, ac_d)], inc=cDV)
        ch_d = cDV.n
        EM("vector", lambda e: e.reduce_sum(lpcol[:, :], chsum[:, :], axis=AX.X),
           wait=[(cDV, ch_d)], inc=cDV)
        lp_d = cDV.n
        EM("sync", lambda e: e.dma_start(out=dram["pubLP"], in_=lpcol[:, :]),
           wait=[(cDV, lp_d)], inc=qS)
        EM("gpsimd", lambda e: e.collective_compute(
            "AllGather", ALU.bypass,
            replica_groups=[[0, 1, 2, 3], [4, 5, 6, 7]],
            ins=[dram["pubLP"]], outs=[dram["gathLP"]]),
           wait=[(qS, qS.n)], inc=cCC)
        EM("sync", lambda e: e.dma_start(
            out=lps[:, :],
            in_=dram["gathLP"].rearrange("(j p) o -> p (j o)", p=128),
            allow_slow_non_contiguous=True),
           wait=[(cCC, cCC.n)], inc=qS)
        lps_d = qS.n
        EM("scalar", lambda e: e.activation(out=tzfm[:, :], in_=chsum[:, :],
                                            func=AF.Exp),
           wait=[(cDV, ch_d)], inc=cAC)
        tz_d = cAC.n
        EM("vector", lambda e: e.tensor_tensor_scan(
            pzfm[:, 1:9], ones128[:, 0:8], chsum[:, :], 0.0, ALU.mult, ALU.add),
           wait=[(cDV, ch_d)], inc=cDV)
        pz_d = cDV.n
        EM("scalar", lambda e: e.activation(out=pzfm[:, 0:8], in_=pzfm[:, 0:8],
                                            func=AF.Exp),
           wait=[(cDV, pz_d)], inc=cAC)
        pz_e = cAC.n
        ev_round = 0
        for rnd in range(2):
            zs = range(rnd * 4, rnd * 4 + 4)
            for z in zs:
                EM("tensor", lambda e, z=z: e.transpose(
                    sm32[:, (z % 4) * 128:(z % 4) * 128 + 128],
                    dts[:, z * 128:z * 128 + 128], id32s[:, :]),
                   wait=[(cDV, dt_d), (cAC, dec_d), (cDV, ev_round)], inc=cPE)
                EM("tensor", lambda e, z=z: e.transpose(
                    st0[:, (z % 4) * 128:(z % 4) * 128 + 128],
                    bc_mr[:, z * 128:z * 128 + 128], id32s[:, :]),
                   wait=[(cDV, ev_round)], inc=cPE)
            trd = cPE.n
            for z in zs:
                EM("vector", lambda e, z=z: e.tensor_copy(
                    out=dtT[:, z * 128:z * 128 + 128],
                    in_=sm32[:, (z % 4) * 128:(z % 4) * 128 + 128]),
                   wait=[(cPE, trd)], inc=cDV)
                EM("vector", lambda e, z=z: e.tensor_copy(
                    out=decT[:, z * 128:z * 128 + 128],
                    in_=st0[:, (z % 4) * 128:(z % 4) * 128 + 128]),
                   wait=[(cPE, trd)], inc=cDV)
            ev_round = cDV.n
        tap_dump("dtT", dtT, 1024)
        def bcast_zh2(src_sb, ncol, dst_sb, waits):
            EM("tensor", lambda e, src_sb=src_sb, ncol=ncol: e.transpose(
                sm32[0:ncol, 0:128], src_sb[:, 0:ncol], id32s[:, :]),
               wait=waits + [(cDV, cDV.n), (cPE, cPE.n)], inc=cPE)
            EM("vector", lambda e: e.tensor_copy(
                out=srow4[0:8, 0:128], in_=sm32[0:8, 0:128]),
               wait=[(cPE, cPE.n)], inc=cDV)
            EM("sync", lambda e: e.dma_start(out=paf[0:1, 0:1024],
                                             in_=srow4[0:8, 0:128]),
               wait=[(cDV, cDV.n)], inc=qS)
            fl = qS.n
            for tt in range(2):
                stp = st0 if tt == 0 else st1
                EM("tensor", lambda e, tt=tt, stp=stp: e.matmul(
                    stp[:, 0:512], ones_f[0:1, 0:128],
                    paf[0:1, tt * 512:tt * 512 + 512], start=True, stop=True),
                   wait=[(qS, fl), (cDV, cDV.n)], inc=cPE)
                EM("vector", lambda e, tt=tt, stp=stp: e.tensor_copy(
                    out=dst_sb[:, tt * 512:tt * 512 + 512], in_=stp[:, 0:512]),
                   wait=[(cPE, cPE.n)], inc=cDV)
        bcast_zh2(tzfm, 8, TB, [(cAC, tz_d)])
        bcast_zh2(pzfm, 8, PB, [(cAC, pz_e)])
        tap_dump("TB", TB, 1024)
        tap_dump("PB", PB, 1024)
        EM("vector", lambda e: e.memset(wlog[:, :], 0.0),
           wait=[(qS, lps_d)], inc=cDV)
        for j in range(4):
            for i in range(4):
                EM("vector", lambda e, j=j, i=i: e.scalar_tensor_tensor(
                    out=wlog[:, j:j + 1], in0=lps[:, i:i + 1],
                    scalar=cco[:, j * 4 + i:j * 4 + i + 1],
                    in1=wlog[:, j:j + 1], op0=ALU.mult, op1=ALU.add),
                   wait=[(cDV, cDV.n)], inc=cDV)
        EM("scalar", lambda e: e.activation(out=wlog[:, :], in_=wlog[:, :],
                                            func=AF.Exp),
           wait=[(cDV, cDV.n)], inc=cAC)
        EM("vector", lambda e: e.tensor_tensor(out=wlog[:, :], in0=wlog[:, :],
                                               in1=mco[:, :], op=ALU.mult),
           wait=[(cAC, cAC.n)], inc=cDV)
        wl_d = cDV.n
        EM("tensor", lambda e: e.transpose(sm32[0:4, 0:128], wlog[:, :],
                                           id32s[:, :]),
           wait=[(cDV, wl_d), (cPE, cPE.n)], inc=cPE)
        EM("vector", lambda e: e.tensor_copy(out=srow4[0:4, 0:128],
                                             in_=sm32[0:4, 0:128]),
           wait=[(cPE, cPE.n)], inc=cDV)
        EM("sync", lambda e: e.dma_start(out=paf[0:1, 0:512],
                                         in_=srow4[0:4, 0:128]),
           wait=[(cDV, cDV.n)], inc=qS)
        fl = qS.n
        EM("tensor", lambda e: e.matmul(st0[:, 0:512], ones_f[0:1, 0:128],
                                        paf[0:1, 0:512], start=True, stop=True),
           wait=[(qS, fl), (cDV, cDV.n)], inc=cPE)
        EM("vector", lambda e: e.tensor_copy(out=WC[:, :], in_=st0[:, 0:512]),
           wait=[(cPE, cPE.n)], inc=cDV)
        tap_dump("WC", WC, 512)
        EM("sync", lambda e: e.dma_start(out=dram["park2"], in_=dram["park1"]),
           inc=qS)
        barrier()

    def mamba_group(g, parts='all'):
        hl0 = g * 16
        gp_st = {}
        oc_order = list(range(8, 18)) + list(range(0, 8))
        slab_mm = {}
        evac_d = {}
        phl_last = 0
        halog_q = 0
        for idx, oc in enumerate(oc_order):
            wb = wWA if idx % 2 == 0 else wWB
            EM("gpsimd", lambda e, g=g, oc=oc, wb=wb: e.dma_start(
                out=wb[:, :],
                in_=dram["Win_t"][g, oc, :, :, :].rearrange("d k o -> d (k o)")),
               wait=[(cPE, slab_mm.get(idx - 2, 0))], inc=qG)
            wld = qG.n
            for tt in range(2):
                pp = [mmA, mmB, psY, psS][(idx % 2) * 2 + tt]
                for kc in range(KD):
                    EM("tensor", lambda e, tt=tt, kc=kc, pp=pp, wb=wb: e.matmul(
                        pp[:, 0:512], wb[:, kc * 128:kc * 128 + 128],
                        arena[:, XN0 + kc * 1024 + tt * 512:XN0 + kc * 1024 + tt * 512 + 512],
                        start=(kc == 0), stop=(kc == KD - 1)),
                       wait=([(qG, wld), (cDV, evac_d.get((idx - 2, tt), 0))]
                             if kc == 0 else []), inc=cPE)
                mmdone = cPE.n
                slab_mm[idx] = mmdone
                if oc >= 8:
                    cc = oc - 8
                    if cc < 8:
                        dst = arena[:, HS0 + cc * 1024 + tt * 512:HS0 + cc * 1024 + tt * 512 + 512]
                    elif cc == 8:
                        dst = Bfm[:, tt * 512:tt * 512 + 512]
                    else:
                        dst = Cfm[:, tt * 512:tt * 512 + 512]
                    EM("vector", lambda e, dst=dst, pp=pp: e.tensor_copy(
                        out=dst, in_=pp[:, 0:512]),
                       wait=[(cPE, mmdone)], inc=cDV)
                else:
                    sl = oc % 2
                    EM("vector", lambda e, tt=tt, sl=sl, pp=pp: e.tensor_copy(
                        out=castbf[:, sl * 1024 + tt * 512:sl * 1024 + tt * 512 + 512],
                        in_=pp[:, 0:512]),
                       wait=[(cPE, mmdone), (qS, gp_st.get(oc - 2, 0))], inc=cDV)
                evac_d[(idx, tt)] = cDV.n
            if oc >= 8:
                cc = oc - 8
                if cc < 8:
                    src3 = arena[:, HS0 + cc * 1024 + 1021:HS0 + cc * 1024 + 1024]
                elif cc == 8:
                    src3 = Bfm[:, 1021:1024]
                else:
                    src3 = Cfm[:, 1021:1024]
                EM("vector", lambda e, cc=cc, src3=src3: e.tensor_copy(
                    out=phbuf[:, cc * 3:cc * 3 + 3], in_=src3),
                   wait=[(cDV, evac_d[(idx, 1)])], inc=cDV)
                phl_last = cDV.n
            else:
                EM("sync", lambda e, g=g, oc=oc, sl=oc % 2: e.dma_start(
                    out=dram["gatepark"][g * 1024 + oc * 128:g * 1024 + oc * 128 + 128, :],
                    in_=castbf[:, sl * 1024:sl * 1024 + 1024]),
                   wait=[(cDV, evac_d[(idx, 1)])], inc=qS)
                gp_st[oc] = qS.n
            if oc == 17:
                EM("sync", lambda e, g=g: e.dma_start(
                    out=dram[f"pubH{g}"], in_=phbuf[:, :]),
                   wait=[(cDV, phl_last)], inc=qS)
                EM("gpsimd", lambda e, g=g: e.collective_compute(
                    "AllGather", ALU.bypass,
                    replica_groups=[[0, 1, 2, 3], [4, 5, 6, 7]],
                    ins=[dram[f"pubH{g}"]], outs=[dram[f"gatH{g}"]]),
                   wait=[(qS, qS.n)], inc=cCC)
                EM("sync", lambda e, g=g: e.dma_start(
                    out=halog[:, :],
                    in_=dram[f"gatH{g}"].rearrange("(r p) w -> p r w", p=128)),
                   wait=[(cCC, cCC.n)], inc=qS)
                halog_q = qS.n
        if parts == 'inproj':
            barrier()
            return
        EM("vector", lambda e: e.tensor_scalar(
            haloc[:, :], halog[:, 0:32], phal[:, 0:1], None, ALU.mult),
           wait=[(qS, halog_q)], inc=cDV)
        for r in range(1, 4):
            EM("vector", lambda e, r=r: e.scalar_tensor_tensor(
                out=haloc[:, :], in0=halog[:, r * 32:r * 32 + 32],
                scalar=phal[:, r:r + 1], in1=haloc[:, :],
                op0=ALU.mult, op1=ALU.add), wait=[(cDV, cDV.n)], inc=cDV)
        haloc_d = cDV.n
        conv_ac = {}
        cva_used = {"vector": 0, "gpsimd": 0}
        for cc in range(10):
            sl = cc % 2
            gc = g * 10 + cc
            eng = "vector"
            ceng = cDV
            if cc < 8:
                src = arena[:, HS0 + cc * 1024:HS0 + cc * 1024 + 1024]
                dst = arena[:, HS0 + cc * 1024:HS0 + cc * 1024 + 1024]
            elif cc == 8:
                src = Bfm[:, :]
                dst = Bfm[:, :]
            else:
                src = Cfm[:, :]
                dst = Cfm[:, :]
            w0 = [(cDV, evac_d[(cc, 1)]), (cDV, phl_last),
                  (cAC, conv_ac.get(cc - 2, 0))]
            EM(eng, lambda e, sl=sl, gc=gc, src=src: e.tensor_scalar(
                cva[:, sl * 1024:sl * 1024 + 1024], src,
                convw[:, gc * 4 + 3:gc * 4 + 4], None, ALU.mult),
               wait=w0, inc=ceng)
            for k in (2, 1, 0):
                o0 = 3 - k
                EM(eng, lambda e, sl=sl, gc=gc, k=k, o0=o0, src=src:
                   e.scalar_tensor_tensor(
                       out=cva[:, sl * 1024 + o0:sl * 1024 + 1024],
                       in0=src[:, 0:1024 - o0],
                       scalar=convw[:, gc * 4 + k:gc * 4 + k + 1],
                       in1=cva[:, sl * 1024 + o0:sl * 1024 + 1024],
                       op0=ALU.mult, op1=ALU.add), inc=ceng)
            EM(eng, lambda e, sl=sl, gc=gc: e.tensor_scalar(
                cva[:, sl * 1024:sl * 1024 + 1024],
                cva[:, sl * 1024:sl * 1024 + 1024],
                convb[:, gc:gc + 1], None, ALU.add), inc=ceng)
            EM(eng, lambda e: e.nop(), wait=[(cDV, haloc_d)])
            for t in range(3):
                for k in range(3 - t):
                    EM(eng, lambda e, sl=sl, gc=gc, cc=cc, t=t, k=k:
                       e.scalar_tensor_tensor(
                           out=cva[:, sl * 1024 + t:sl * 1024 + t + 1],
                           in0=haloc[:, cc * 3 + t + k:cc * 3 + t + k + 1],
                           scalar=convw[:, gc * 4 + k:gc * 4 + k + 1],
                           in1=cva[:, sl * 1024 + t:sl * 1024 + t + 1],
                           op0=ALU.mult, op1=ALU.add), inc=ceng)
                    EM(eng, lambda e: e.nop(), wait=[(ceng, ceng.n)])
            chain_d = ceng.n
            EM("scalar", lambda e, sl=sl, dst=dst: e.activation(
                out=dst, in_=cva[:, sl * 1024:sl * 1024 + 1024], func=AF.Silu),
               wait=[(ceng, chain_d)], inc=cAC)
            conv_ac[cc] = cAC.n
        conv_done = cAC.n
        if g == 0:
            tap_dump("hs0", arena[:, HS0:HS0 + 8192], 8192)
            tap_dump("b0", Bfm, 1024)
            tap_dump("c0", Cfm, 1024)
        if parts == 'conv':
            barrier()
            return
        EM("vector", lambda e: e.memset(bc_r[:, :], 0.0), inc=cDV)
        scar_init = cDV.n
        ph0_dv = cDV.n
        for z in range(NZ):
            EM("tensor", lambda e, z=z: e.transpose(
                psT[:, z * 128:z * 128 + 128], Bfm[:, z * 128:z * 128 + 128],
                idbfs[:, :]),
               wait=[(cAC, conv_ac[8]), (cDV, ph0_dv)], inc=cPE)
        btr = cPE.n
        for z in range(NZ):
            EM("vector", lambda e, z=z: e.tensor_copy(
                out=Btm[:, z * 128:z * 128 + 128],
                in_=psT[:, z * 128:z * 128 + 128]),
               wait=[(cPE, btr)], inc=cDV)
        btm_d = cDV.n
        BANKS = [sm32, st0, psY, psS]
        d_mm = {}
        dexp = {}
        m_dd = {}
        ydmm = {}
        stmm = {}
        scd = {}
        ydev = {}
        xd_dd = {}
        paf_used = {}
        bcm_used = {}
        zpe = {}
        zdv = {}
        xtm_ev = {}
        for z in range(NZ):
            zp = z % 2
            EM("sync", lambda e, g=g, z=z: e.dma_start(
                out=paf[0:1, 0:1024],
                in_=acum[g * 16:g * 16 + 8, z * 128:z * 128 + 128]),
               wait=[(cPE, paf_used.get(z - 1, 0))], inc=qS)
            EM("sync", lambda e, g=g, z=z: e.dma_start(
                out=paf[64:65, 0:1024],
                in_=acum[g * 16 + 8:g * 16 + 16, z * 128:z * 128 + 128]),
               inc=qS)
            paf_d = qS.n
            bmk = mmA if zp == 0 else mmB
            EM("tensor", lambda e, z=z, bmk=bmk: e.matmul(
                bmk[:, 0:128],
                Bfm[:, z * 128:z * 128 + 128], Cfm[:, z * 128:z * 128 + 128],
                start=True, stop=True),
               wait=[(cAC, conv_ac[9]), (cDV, bcm_used.get(z - 2, ph0_dv))],
               inc=cPE)
            EM("vector", lambda e, zp=zp, bmk=bmk: e.tensor_tensor(
                out=bcmb[:, zp * 128:zp * 128 + 128],
                in0=bmk[:, 0:128],
                in1=trius[:, :], op=ALU.mult), wait=[(cPE, cPE.n)], inc=cDV)
            bcm_d = cDV.n
            bcm_used[z] = bcm_d
            for hc in range(8):
                EM("tensor", lambda e, z=z, hc=hc: e.transpose(
                    psT[:, hc * 128:hc * 128 + 128],
                    arena[:, HS0 + hc * 1024 + z * 128:HS0 + hc * 1024 + z * 128 + 128],
                    idbfs[:, :]),
                   wait=[(cAC, conv_ac[hc]),
                         (cDV, xtm_ev.get(z - 1, btm_d))], inc=cPE)
            trd = cPE.n
            for hc in range(8):
                for hf in range(2):
                    hl = hc * 2 + hf
                    EM("vector", lambda e, z=z, zp=zp, hc=hc, hf=hf, hl=hl:
                       e.tensor_scalar(
                           xtmb[:, zp * 1024 + hc * 128 + hf * 64:zp * 1024 + hc * 128 + hf * 64 + 64],
                           psT[:, hc * 128 + hf * 64:hc * 128 + hf * 64 + 64],
                           dtT[:, z * 128 + hl0 + hl:z * 128 + hl0 + hl + 1],
                           None, ALU.mult),
                       wait=[(cPE, trd), (cPE, zpe.get(z - 2, 0))], inc=cDV)
            xtm_ev[z] = cDV.n
            xtm_d = cDV.n
            for hl in range(16):
                li = z * 16 + hl
                h = hl0 + hl
                s2, s4 = hl % 2, hl % 4
                pb = (hl % 2) * 64
                bk = BANKS[li % 4]
                pr = (hl // 8) * 64
                EM("tensor", lambda e, hl=hl, bk=bk, pr=pr: e.matmul(
                    bk[:, 0:128],
                    paf[pr:pr + 1, (hl % 8) * 128:(hl % 8) * 128 + 128],
                    ones128[pr:pr + 1, 0:128], start=True, stop=False),
                   wait=[(qS, paf_d), (cAC, dexp.get(li - 4, 0)),
                         (cDV, ydev.get(li - 4, ph0_dv))], inc=cPE)
                EM("tensor", lambda e, hl=hl, bk=bk, pr=pr: e.matmul(
                    bk[:, 0:128], neg128[pr:pr + 1, 0:128],
                    paf[pr:pr + 1, (hl % 8) * 128:(hl % 8) * 128 + 128],
                    start=False, stop=True), inc=cPE)
                dmm = cPE.n
                EM("scalar", lambda e, s2=s2, bk=bk: e.activation(
                    out=e1s[:, s2 * 128:s2 * 128 + 128],
                    in_=bk[:, 0:128], func=AF.Relu),
                   wait=[(cPE, dmm)], inc=cAC)
                EM("scalar", lambda e, s2=s2: e.activation(
                    out=e1s[:, s2 * 128:s2 * 128 + 128],
                    in_=e1s[:, s2 * 128:s2 * 128 + 128], func=AF.Exp,
                    scale=-1.0), wait=[(cPL, m_dd.get(li - 2, 0))], inc=cAC)
                dexp[li] = cAC.n
                EM("gpsimd", lambda e, s4=s4, s2=s2, zp=zp: e.tensor_tensor(
                    out=msb[:, s4 * 128:s4 * 128 + 128],
                    in0=e1s[:, s2 * 128:s2 * 128 + 128],
                    in1=bcmb[:, zp * 128:zp * 128 + 128], op=ALU.mult),
                   wait=[(cAC, dexp[li]), (cDV, bcm_d),
                         (cPE, ydmm.get(li - 4, 0))], inc=cPL)
                m_dd[li] = cPL.n
                EM("vector", lambda e, zp=zp, hl=hl, s4=s4, z=z: e.tensor_scalar(
                    xds[:, s4 * 64:s4 * 64 + 64],
                    xtmb[:, zp * 1024 + (hl // 2) * 128 + (hl % 2) * 64:
                         zp * 1024 + (hl // 2) * 128 + (hl % 2) * 64 + 64],
                    decT[:, z * 128 + hl0 + hl:z * 128 + hl0 + hl + 1],
                    None, ALU.mult),
                   wait=[(cPE, stmm.get(li - 4, 0))], inc=cDV)
                xd_d = cDV.n
                EM("tensor", lambda e, zp=zp, hl=hl, s4=s4, pb=pb, bk=bk:
                   e.matmul(
                       bk[pb:pb + 64, 128:256],
                       xtmb[:, zp * 1024 + (hl // 2) * 128 + (hl % 2) * 64:
                            zp * 1024 + (hl // 2) * 128 + (hl % 2) * 64 + 64],
                       msb[:, s4 * 128:s4 * 128 + 128],
                       start=True, stop=True),
                   wait=[(cPL, m_dd[li])], inc=cPE)
                ydmm[li] = cPE.n
                EM("tensor", lambda e, z=z, s4=s4, bk=bk: e.matmul(
                    bk[:, 256:320],
                    Btm[:, z * 128:z * 128 + 128],
                    xds[:, s4 * 64:s4 * 64 + 64], start=True, stop=True),
                   wait=[(cDV, xd_d), (cDV, btm_d)], inc=cPE)
                stmm[li] = cPE.n
                EM("vector", lambda e, z=z, hl=hl: e.tensor_copy(
                    out=arena[:, SH0 + (z * 16 + hl) * 64:SH0 + (z * 16 + hl) * 64 + 64],
                    in_=bc_r[:, hl * 64:hl * 64 + 64]),
                   wait=[(cDV, scar_init)], inc=cDV)
                EM("vector", lambda e, hl=hl, z=z, h=h, bk=bk:
                   e.scalar_tensor_tensor(
                       out=bc_r[:, hl * 64:hl * 64 + 64],
                       in0=bc_r[:, hl * 64:hl * 64 + 64],
                       scalar=TB[:, z * 128 + h:z * 128 + h + 1],
                       in1=bk[:, 256:320],
                       op0=ALU.mult, op1=ALU.add),
                   wait=[(cPE, stmm[li])], inc=cDV)
                scd[li] = cDV.n
                EM("vector", lambda e, z=z, hl=hl, pb=pb, h=h, bk=bk:
                   e.scalar_tensor_tensor(
                       out=arena[pb:pb + 64, HS0 + (hl // 2) * 1024 + z * 128:HS0 + (hl // 2) * 1024 + z * 128 + 128],
                       in0=arena[pb:pb + 64, HS0 + (hl // 2) * 1024 + z * 128:HS0 + (hl // 2) * 1024 + z * 128 + 128],
                       scalar=DB[pb:pb + 64, h:h + 1],
                       in1=bk[pb:pb + 64, 128:256],
                       op0=ALU.mult, op1=ALU.add),
                   wait=[(cPE, stmm[li])], inc=cDV)
                ydev[li] = cDV.n
            paf_used[z] = cPE.n
            zpe[z] = cPE.n
        ph1_dv = cDV.n
        if g == 0:
            tap_dump("sh0", arena[:, SH0:SH0 + 8192], 8192)
        if parts == 'ph1':
            barrier()
            return
        EM("gpsimd", lambda e, g=g: e.dma_start(out=dram[f"pubF{g}"],
                                                in_=bc_r[:, :]),
           wait=[(cDV, ph1_dv)], inc=qG)
        EM("gpsimd", lambda e, g=g: e.collective_compute(
            "AllGather", ALU.bypass,
            replica_groups=[[0, 1, 2, 3], [4, 5, 6, 7]],
            ins=[dram[f"pubF{g}"]], outs=[dram[f"gatF{g}"]]),
           wait=[(qG, qG.n)], inc=cCC)
        cc_f = cCC.n
        EM("vector", lambda e: e.memset(bc_mr[:, :], 0.0), inc=cDV)
        fs_used = {}
        for j in range(4):
            sl = j % 2
            EM("sync", lambda e, g=g, j=j, sl=sl: e.dma_start(
                out=xtmb[:, sl * 1024:sl * 1024 + 1024],
                in_=dram[f"gatF{g}"][j * 128:j * 128 + 128, :]),
               wait=[(cCC, cc_f), (cDV, fs_used.get(j - 2, ph1_dv)),
                     (cPE, zpe[NZ - 1])], inc=qS)
            fld = qS.n
            for hl in range(16):
                h = hl0 + hl
                EM("vector", lambda e, j=j, sl=sl, hl=hl, h=h:
                   e.scalar_tensor_tensor(
                       out=bc_mr[:, hl * 64:hl * 64 + 64],
                       in0=xtmb[:, sl * 1024 + hl * 64:sl * 1024 + hl * 64 + 64],
                       scalar=WC[:, j * 128 + h:j * 128 + h + 1],
                       in1=bc_mr[:, hl * 64:hl * 64 + 64],
                       op0=ALU.mult, op1=ALU.add),
                   wait=[(qS, fld)], inc=cDV)
            fs_used[j] = cDV.n
        icomb_d = cDV.n
        if g == 0:
            tap_dump("ib0", bc_mr, 1024)
        eax = {}
        yomm = {}
        yoev = {}
        cedd = {}
        ea_used = {}
        for z in range(NZ):
            EM("sync", lambda e, g=g, z=z: e.dma_start(
                out=paf[0:1, 0:1024],
                in_=acum[g * 16:g * 16 + 8, z * 128:z * 128 + 128]),
               wait=[(cPE, ea_used.get(z - 1, zpe[NZ - 1]))], inc=qS)
            EM("sync", lambda e, g=g, z=z: e.dma_start(
                out=paf[64:65, 0:1024],
                in_=acum[g * 16 + 8:g * 16 + 16, z * 128:z * 128 + 128]),
               inc=qS)
            paf_d = qS.n
            for hl in range(16):
                li = z * 16 + hl
                h = hl0 + hl
                s2, s4 = hl % 2, hl % 4
                pb = (hl % 2) * 64
                ebk = sm32 if s2 == 0 else st0
                ybk = psY if s2 == 0 else psS
                pr = (hl // 8) * 64
                EM("vector", lambda e, z=z, hl=hl, h=h: e.scalar_tensor_tensor(
                    out=arena[:, SH0 + (z * 16 + hl) * 64:SH0 + (z * 16 + hl) * 64 + 64],
                    in0=bc_mr[:, hl * 64:hl * 64 + 64],
                    scalar=PB[:, z * 128 + h:z * 128 + h + 1],
                    in1=arena[:, SH0 + (z * 16 + hl) * 64:SH0 + (z * 16 + hl) * 64 + 64],
                    op0=ALU.mult, op1=ALU.add),
                   wait=[(cDV, icomb_d)], inc=cDV)
                corr_d = cDV.n
                EM("tensor", lambda e, hl=hl, ebk=ebk, pr=pr: e.matmul(
                    ebk[:, 320:448],
                    ones128[pr:pr + 1, 0:128],
                    paf[pr:pr + 1, (hl % 8) * 128:(hl % 8) * 128 + 128],
                    start=True, stop=True),
                   wait=[(qS, paf_d), (cAC, eax.get(li - 2, 0)),
                         (cDV, yoev.get(li - 2, ph1_dv))], inc=cPE)
                ea_mm = cPE.n
                EM("scalar", lambda e, s2=s2, ebk=ebk: e.activation(
                    out=e1s[:, s2 * 128:s2 * 128 + 128],
                    in_=ebk[:, 320:448],
                    func=AF.Exp),
                   wait=[(cPE, ea_mm), (cPL, cedd.get(li - 2, 0))], inc=cAC)
                eax[li] = cAC.n
                EM("gpsimd", lambda e, z=z, s2=s2, s4=s4: e.tensor_tensor(
                    out=ces[:, s4 * 128:s4 * 128 + 128],
                    in0=Cfm[:, z * 128:z * 128 + 128],
                    in1=e1s[:, s2 * 128:s2 * 128 + 128], op=ALU.mult),
                   wait=[(cAC, eax[li]), (cPE, yomm.get(li - 4, 0))], inc=cPL)
                cedd[li] = cPL.n
                EM("tensor", lambda e, z=z, hl=hl, s4=s4, pb=pb, ybk=ybk:
                   e.matmul(
                       ybk[pb:pb + 64, 320:448],
                       arena[:, SH0 + (z * 16 + hl) * 64:SH0 + (z * 16 + hl) * 64 + 64],
                       ces[:, s4 * 128:s4 * 128 + 128], start=True, stop=True),
                   wait=[(cPL, cedd[li]), (cDV, corr_d),
                         (cDV, yoev.get(li - 2, ph1_dv))], inc=cPE)
                yomm[li] = cPE.n
                EM("vector", lambda e, z=z, hl=hl, pb=pb, ybk=ybk: e.tensor_tensor(
                    out=arena[pb:pb + 64, HS0 + (hl // 2) * 1024 + z * 128:HS0 + (hl // 2) * 1024 + z * 128 + 128],
                    in0=arena[pb:pb + 64, HS0 + (hl // 2) * 1024 + z * 128:HS0 + (hl // 2) * 1024 + z * 128 + 128],
                    in1=ybk[pb:pb + 64, 320:448],
                    op=ALU.add),
                   wait=[(cPE, yomm[li])], inc=cDV)
                yoev[li] = cDV.n
            ea_used[z] = cPE.n
        ph2_dv = cDV.n
        if g == 0:
            tap_dump("y0", arena[:, HS0:HS0 + 8192], 8192)
        if parts == 'ph2':
            barrier()
            return
        gat_d = {}
        gat_rd = {}
        sqg_mm = {}
        for hc in range(8):
            sl = hc % 2
            EM("sync", lambda e, g=g, hc=hc, sl=sl: e.dma_start(
                out=castbf[:, sl * 1024:sl * 1024 + 1024],
                in_=dram["gatepark"][g * 1024 + hc * 128:g * 1024 + hc * 128 + 128, :]),
               wait=[(qS, gp_st[hc]), (cPL, gat_rd.get(hc - 2, 0))], inc=qS)
            gld = qS.n
            EM("scalar", lambda e, sl=sl: e.activation(
                out=castbf[:, sl * 1024:sl * 1024 + 1024],
                in_=castbf[:, sl * 1024:sl * 1024 + 1024], func=AF.Silu),
               wait=[(qS, gld)], inc=cAC)
            sil_d = cAC.n
            gat_d[hc] = sil_d
            EM("gpsimd", lambda e, hc=hc, sl=sl: e.tensor_tensor(
                out=arena[:, HS0 + hc * 1024:HS0 + hc * 1024 + 1024],
                in0=arena[:, HS0 + hc * 1024:HS0 + hc * 1024 + 1024],
                in1=castbf[:, sl * 1024:sl * 1024 + 1024], op=ALU.mult),
               wait=[(cAC, sil_d), (cDV, ph2_dv)], inc=cPL)
            gtd = cPL.n
            gat_rd[hc] = gtd
            EM("scalar", lambda e, hc=hc, gtd=gtd: e.activation(
                out=sqbf[:, 0:1024], in_=arena[:, HS0 + hc * 1024:HS0 + hc * 1024 + 1024],
                func=AF.Square),
               wait=[(cPL, gtd), (cPE, sqg_mm.get(hc - 1, 0))], inc=cAC)
            sq_d = cAC.n
            for tt in range(2):
                stp = st0 if tt == 0 else st1
                EM("tensor", lambda e, hc=hc, tt=tt, stp=stp: e.matmul(
                    stp[0:1, 0:512], onescbf[:, 0:1],
                    sqbf[:, tt * 512:tt * 512 + 512],
                    start=(hc == 0), stop=(hc == 7)),
                   wait=[(cAC, sq_d)], inc=cPE)
            sqg_mm[hc] = cPE.n
        gst_pe = cPE.n
        for tt in range(2):
            stp = st0 if tt == 0 else st1
            EM("vector", lambda e, tt=tt, stp=stp: e.tensor_copy(
                out=srow4[0:1, tt * 512:tt * 512 + 512], in_=stp[0:1, 0:512]),
               wait=[(cPE, gst_pe)], inc=cDV)
        growd = cDV.n
        for tt in range(2):
            stp = st0 if tt == 0 else st1
            EM("tensor", lambda e, tt=tt, stp=stp: e.matmul(
                stp[:, 0:512], ones_f[0:1, 0:128],
                srow4[0:1, tt * 512:tt * 512 + 512], start=True, stop=True),
               wait=[(cDV, growd)], inc=cPE)
            EM("vector", lambda e, tt=tt, stp=stp: e.tensor_scalar(
                bc_r[:, tt * 512:tt * 512 + 512], stp[:, 0:512],
                1.0 / 1024, EPS, ALU.mult, ALU.add),
               wait=[(cPE, cPE.n)], inc=cDV)
        EM("scalar", lambda e: e.activation(out=bc_r[:, :], in_=bc_r[:, :],
                                            func=AF.Sqrt),
           wait=[(cDV, cDV.n)], inc=cAC)
        EM("vector", lambda e: e.reciprocal(bc_r[:, :], bc_r[:, :]),
           wait=[(cAC, cAC.n)], inc=cDV)
        rb_d = cDV.n
        gat_all = cPL.n
        for hc in range(8):
            for tt in range(2):
                EM("vector", lambda e, g=g, hc=hc, tt=tt:
                   e.scalar_tensor_tensor(
                       out=arena[:, HS0 + hc * 1024 + tt * 512:HS0 + hc * 1024 + tt * 512 + 512],
                       in0=arena[:, HS0 + hc * 1024 + tt * 512:HS0 + hc * 1024 + tt * 512 + 512],
                       scalar=nrmw[:, g * 8 + hc:g * 8 + hc + 1],
                       in1=bc_r[:, tt * 512:tt * 512 + 512],
                       op0=ALU.mult, op1=ALU.mult),
                   wait=[(cDV, rb_d), (cPL, gat_all)], inc=cDV)
        ynorm_d = cDV.n
        if g == 0:
            tap_dump("yn0", arena[:, HS0:HS0 + 8192], 8192)
        op_mm = {}
        op_ev = {}
        for ob in range(8):
            wb = wWA if ob % 2 == 0 else wWB
            EM("gpsimd", lambda e, g=g, ob=ob, wb=wb: e.dma_start(
                out=wb[:, :],
                in_=dram["Wout_t"][g, ob, :, :, :, :].rearrange("d k c o -> d (k c o)")),
               wait=[(cPE, op_mm.get(ob - 2, 0))], inc=qG)
            wld = qG.n
            for oci in range(4):
                oc = ob * 4 + oci
                for tt in range(2):
                    pp = [mmA, mmB, psY, psS][(oc % 2) * 2 + tt]
                    for kc in range(8):
                        EM("tensor", lambda e, oci=oci, tt=tt, kc=kc, pp=pp,
                           wb=wb: e.matmul(
                               pp[:, 0:512],
                               wb[:, (kc * 4 + oci) * 128:(kc * 4 + oci) * 128 + 128],
                               arena[:, HS0 + kc * 1024 + tt * 512:HS0 + kc * 1024 + tt * 512 + 512],
                               start=(kc == 0), stop=(kc == 7)),
                           wait=([(qG, wld), (cDV, ynorm_d),
                                  (cDV, op_ev.get((oc - 1, tt), 0))]
                                 if kc == 0 else []), inc=cPE)
                    mmdone = cPE.n
                    op_mm[ob] = mmdone
                    sl = tt
                    EM("vector", lambda e, sl=sl, pp=pp: e.tensor_copy(
                        out=xin32[:, sl * 512:sl * 512 + 512], in_=pp[:, 0:512]),
                       wait=[(cPE, mmdone), (qG, op_st.get((oc - 1, tt), 0))],
                       inc=cDV)
                    evd = cDV.n
                    op_ev[(oc, tt)] = evd
                    EM("gpsimd", lambda e, oc=oc, tt=tt, sl=sl: e.dma_start(
                        out=dram["park2"][oc * 128:oc * 128 + 128,
                                          tt * 512:tt * 512 + 512],
                        in_=xin32[:, sl * 512:sl * 512 + 512],
                        accum_op=ALU.add),
                       wait=[(cDV, evd)], inc=qG)
                    op_st[(oc, tt)] = qG.n
        barrier()

    op_st = {}

    def idbf_id(e):
        return id32s[:, :]

    ln_phase("xT")
    if isinstance(do_ffn1, str):
        ffn_phase(1, parts=do_ffn1)
    elif do_ffn1:
        ffn_phase(1)
    else:
        EM("sync", lambda e: e.dma_start(out=dram["park1"], in_=dram["xT"]),
           inc=qS)
        barrier()
    ln_phase("park1")
    tap_dump("xn", arena[:, 0:32768], 32768)
    if do_mamba:
        dt_phase()
        if do_mamba == "dt":
            pass
        elif isinstance(do_mamba, str):
            for g in range(n_groups):
                mamba_group(g, parts=do_mamba)
        else:
            for g in range(n_groups):
                mamba_group(g)
    else:
        EM("sync", lambda e: e.dma_start(out=dram["park2"], in_=dram["park1"]),
           inc=qS)
        barrier()
    ln_phase("park2")
    if do_ffn2:
        ffn_phase(2)
    else:
        for kc in range(KD):
            EM("sync", lambda e, kc=kc: e.dma_start(
                out=outT[kc * 128:kc * 128 + 128, :],
                in_=dram["park2"][kc * 128:kc * 128 + 128, :]), inc=qS)
        barrier()
    fin = tuple((c, c.n) for c in ALLC if c.n > 0)
    def run_fin(eng, fin=fin):
        for c, v in fin:
            eng.wait_ge(c.sem, v)
    pl.ops["sync"].append(run_fin)

    with ctx:
        with nc.Block() as block:
            @block.sync
            def _(eng):
                for f in pl.ops["sync"]:
                    f(eng)

            @block.gpsimd
            def _(eng):
                for f in pl.ops["gpsimd"]:
                    f(eng)

            @block.tensor
            def _(eng):
                for f in pl.ops["tensor"]:
                    f(eng)

            @block.vector
            def _(eng):
                for f in pl.ops["vector"]:
                    f(eng)

            @block.scalar
            def _(eng):
                for f in pl.ops["scalar"]:
                    f(eng)
    return nc


def _prep_shared(i):
    f32 = np.float32
    out = {}
    Wb = np.asarray(i["L0_ffn_W_base"], f32)
    out["Wb_t"] = np.ascontiguousarray(
        Wb.T.reshape(32, 128, 32, 128).transpose(2, 1, 0, 3))
    out["bbimg"] = np.ascontiguousarray(
        np.asarray(i["L0_ffn_b_base"], f32).reshape(32, 128).T)
    Aw = np.asarray(i["L0_ffn_A_w"], f32)
    AcT = Aw.reshape(128, 4096).T
    out["Acat_img"] = np.ascontiguousarray(
        AcT.reshape(32, 128, 128).transpose(1, 0, 2).reshape(128, 4096))
    Bw = np.asarray(i["L0_ffn_B_w"], f32)
    out["Bcat_img"] = np.ascontiguousarray(
        Bw.transpose(0, 2, 1).reshape(128, 4096))
    Wg9 = np.concatenate([np.asarray(i["L0_ffn_W_gate"], f32),
                          np.asarray(i["L0_ffn_W_thr"], f32)[None]], 0)
    out["Wg9_img"] = np.ascontiguousarray(
        Wg9.T.reshape(32, 128, 9).transpose(1, 0, 2).reshape(128, 288))
    out["bthr_col"] = np.full((128, 1), np.asarray(i["L0_ffn_b_thr"], f32)[0],
                              f32)
    Win = np.asarray(i["L0_attn_in_proj_w"], f32)
    wint = np.empty((G, 18, 128, 32, 128), f32)
    for g in range(G):
        rows = np.concatenate([
            np.arange(8192 + g * 1024, 8192 + (g + 1) * 1024),
            np.arange(16384 + g * 128, 16384 + (g + 1) * 128),
            np.arange(17408 + g * 128, 17408 + (g + 1) * 128),
        ])
        rows = np.concatenate([np.arange(g * 1024, (g + 1) * 1024), rows])
        WgT = Win[rows].T
        wint[g] = WgT.reshape(32, 128, 18, 128).transpose(2, 1, 0, 3)
    out["Win_t"] = wint
    Wdt = Win[18432:18560]
    out["Wdt_img"] = np.ascontiguousarray(
        Wdt.T.reshape(32, 128, 128).transpose(1, 0, 2).reshape(128, 4096))
    Wout = np.asarray(i["L0_attn_out_proj_w"], f32)
    WoT = Wout.T
    out["Wout_t"] = np.ascontiguousarray(
        WoT.reshape(G, 8, 128, 8, 4, 128).transpose(0, 3, 2, 1, 4, 5))
    convw = np.asarray(i["L0_attn_conv_w"], f32)[:, 0, :]
    convb = np.asarray(i["L0_attn_conv_b"], f32)
    cwi = np.empty((128, 320), f32)
    cbi = np.empty((128, 80), f32)
    for g in range(G):
        for c in range(10):
            if c < 8:
                ch = g * 1024 + c * 128 + np.arange(128)
            elif c == 8:
                ch = 8192 + g * 128 + np.arange(128)
            else:
                ch = 9216 + g * 128 + np.arange(128)
            cwi[:, (g * 10 + c) * 4:(g * 10 + c) * 4 + 4] = convw[ch]
            cbi[:, g * 10 + c] = convb[ch]
    out["convw_img"] = cwi
    out["convb_img"] = cbi
    out["dtb_col"] = np.asarray(i["L0_attn_dt_bias"], f32).reshape(128, 1)
    out["Alog_col"] = np.asarray(i["L0_attn_A_log"], f32).reshape(128, 1)
    out["D_col"] = np.asarray(i["L0_attn_D"], f32).reshape(128, 1)
    out["normw_img"] = np.ascontiguousarray(
        np.asarray(i["L0_attn_norm_w"], f32).reshape(64, 128).T)
    out["id32"] = np.eye(128, dtype=f32)
    out["triu"] = np.triu(np.ones((128, 128), f32))
    rep = np.zeros((8, 128), f32)
    for e in range(8):
        rep[e, e * 16:(e + 1) * 16] = 1.0
    out["repmat"] = rep
    return out


def _prep_core(i, k, shared):
    f32 = np.float32
    x = np.asarray(i["L0_attn_x"], f32)
    b, q = k // 4, k % 4
    d = dict(shared)
    d["xT"] = np.ascontiguousarray(x[b, q * T:(q + 1) * T, :].T)
    ph = np.zeros((128, 4), f32)
    if q != 0:
        ph[:, q - 1] = 1.0
    d["phalo"] = ph
    mc = np.zeros((128, 4), f32)
    mc[:, :q] = 1.0
    d["mcoef"] = mc
    cc = np.zeros((128, 16), f32)
    for j in range(4):
        for ii in range(4):
            if j < ii < q:
                cc[:, j * 4 + ii] = 1.0
    d["ccoef"] = cc
    return d


def kernel(**inputs):
    shared = _prep_shared(inputs)
    in_maps = [_prep_core(inputs, k, shared) for k in range(8)]
    nc = build_program()
    res = run_bass_kernel_spmd(nc, in_maps, core_ids=list(range(8)))
    x = np.asarray(inputs["L0_attn_x"])
    out = np.empty((2, 4096, 4096), np.float32)
    for k in range(8):
        b, q = k // 4, k % 4
        out[b, q * T:(q + 1) * T, :] = res.results[k]["outT"].T
    return out


# version 04 (final)
# speedup vs baseline: 1.3037x; 1.3037x over previous
import numpy as np
import ml_dtypes
import concourse.bass as bass
import concourse.mybir as mybir
from concourse.bass_utils import run_bass_kernel_spmd
from contextlib import ExitStack

F32 = mybir.dt.float32
BF16 = mybir.dt.bfloat16
AF = mybir.ActivationFunctionType
ALU = mybir.AluOpType
AX = mybir.AxisListType

T = 1024
KD = 32
G = 8
NZ = 8
EPS = 1e-5


class Ctr:
    def __init__(self, sem, step=1):
        self.sem, self.n, self.step = sem, 0, step


class Planner:
    ENGS = ("sync", "gpsimd", "tensor", "vector", "scalar")

    def __init__(self):
        self.ops = {e: [] for e in self.ENGS}

    def emit(self, ename, fn, wait=(), inc=None):
        ws = tuple((c.sem, v) for (c, v) in wait if v > 0)
        isem = iamt = None
        if inc is not None:
            inc.n += inc.step
            isem, iamt = inc.sem, inc.step

        def run(eng, ws=ws, fn=fn, isem=isem, iamt=iamt):
            for sem, v in ws:
                eng.wait_ge(sem, v)
            inst = fn(eng)
            if isem is not None:
                inst.then_inc(isem, iamt)
        self.ops[ename].append(run)


def build_program(debug_taps=(), n_groups=G, do_ffn1=True, do_mamba=True,
                  do_ffn2=True):
    nc = bass.Bass("TRN2", target_bir_lowering=False, debug=False)
    dram = {}

    def din(name, shape, dt=F32):
        dram[name] = nc.dram_tensor(name, shape, dt, kind="ExternalInput").ap()

    def dint(name, shape, dt=F32):
        dram[name] = nc.dram_tensor(name, shape, dt).ap()

    din("xT", [4096, T])
    din("Wb_t", [32, 128, 32, 128])
    din("bbimg", [128, 32])
    din("Acat_img", [128, 4096])
    din("Bcat_img", [128, 4096])
    din("Wg9_img", [128, 288])
    din("bthr_col", [128, 1])
    din("Win_t", [G, 18, 128, 32, 128])
    din("Wdt_img", [128, 4096])
    din("Wout_t", [G, 8, 128, 8, 4, 128])
    din("convw_img", [128, 320])
    din("convb_img", [128, 80])
    din("dtb_col", [128, 1])
    din("Alog_col", [128, 1])
    din("D_col", [128, 1])
    din("normw_img", [128, 64])
    din("id32", [128, 128])
    din("triu", [128, 128])
    din("phalo", [128, 4])
    din("mcoef", [128, 4])
    din("ccoef", [128, 16])
    din("repmat", [8, 128])
    outT = nc.dram_tensor("outT", [4096, T], F32, kind="ExternalOutput").ap()
    tapshapes = {
        "xn": [128, KD * 1024], "dt": [128, T], "acum": [128, T],
        "hs0": [128, 8192], "b0": [128, 1024], "c0": [128, 1024],
        "y0": [128, 8192], "yn0": [128, 8192], "sh0": [128, 8192],
        "hw": [128, 1024], "z9": [128, 72], "dtT": [128, 1024],
        "TB": [128, 1024], "PB": [128, 1024], "WC": [128, 512],
        "ib0": [128, 1024],
    }
    taps = {}
    for tp in debug_taps:
        taps[tp] = nc.dram_tensor("tap_" + tp, tapshapes[tp], F32,
                                  kind="ExternalOutput").ap()

    dint("park1", [4096, T])
    dint("park2", [4096, T])
    dint("gatepark", [G * 1024, T], BF16)
    dint("pubLP", [128, 1])
    dint("gathLP", [512, 1])
    for g in range(G):
        dint(f"pubH{g}", [128, 32], BF16)
        dint(f"gatH{g}", [512, 32], BF16)
        dint(f"pubF{g}", [128, 1024], BF16)
        dint(f"gatF{g}", [512, 1024], BF16)

    ctx = ExitStack()
    sbuf = lambda nm, sh, dt=F32: ctx.enter_context(nc.sbuf_tensor(nm, sh, dt))
    psum = lambda nm, sh, dt=F32: ctx.enter_context(nc.psum_tensor(nm, sh, dt))

    arena = sbuf("arena", [128, 53248], BF16)

    def asl(lo, hi):
        return arena[:, lo:hi]
    XN0 = 0
    HS0 = 32768
    SH0 = 40960
    BC0 = 49152

    wWA = sbuf("wWA", [128, 4096], BF16)
    wWB = sbuf("wWB", [128, 4096], BF16)
    Wg9 = sbuf("Wg9", [128, 288], BF16)
    bbimg = sbuf("bbimg_s", [128, 32])
    halfb = sbuf("halfb", [128, 32])
    convw = sbuf("convw_s", [128, 320])
    convb = sbuf("convb_s", [128, 80])
    dtbc = sbuf("dtbc", [128, 1])
    alogc = sbuf("alogc", [128, 1])
    acolc = sbuf("acolc", [128, 1])
    dcolc = sbuf("dcolc", [128, 1])
    nrmw = sbuf("nrmw", [128, 64])
    id32s = sbuf("id32s", [128, 128])
    idbfs = sbuf("idbfs", [128, 128], BF16)
    trius = sbuf("trius", [128, 128])
    phal = sbuf("phal", [128, 4])
    mco = sbuf("mco", [128, 4])
    cco = sbuf("cco", [128, 16])
    bthr = sbuf("bthr", [128, 1])
    repm = sbuf("repm", [8, 128])
    ones_f = sbuf("ones_f", [1, 128])
    neg128 = sbuf("neg128", [128, 128])
    onescbf = sbuf("onescbf", [128, 1], BF16)
    ones128 = sbuf("ones128", [128, 128])

    xin32 = sbuf("xin32", [128, 2048])
    castbf = sbuf("castbf", [128, 2048], BF16)
    sqbf = sbuf("sqbf", [128, 1024], BF16)
    srow4 = sbuf("srow4", [128, 1024])
    bc_mr = sbuf("bc_mr", [128, 1024])
    bc_r = sbuf("bc_r", [128, 1024])

    dts = sbuf("dts", [128, T])
    acum = sbuf("acum", [128, T])
    chsum = sbuf("chsum", [128, 8])
    tzfm = sbuf("tzfm", [128, 8])
    pzfm = sbuf("pzfm", [128, 9])
    dtT = sbuf("dtT", [128, 1024])
    decT = sbuf("decT", [128, 1024])
    TB = sbuf("TB", [128, 1024])
    PB = sbuf("PB", [128, 1024])
    DB = sbuf("DB", [128, 128])
    WC = sbuf("WC", [128, 512])
    lps = sbuf("lps", [128, 4])
    wlog = sbuf("wlog", [128, 4])
    lpcol = sbuf("lpcol", [128, 1])
    paf = sbuf("paf", [128, 1024])

    cva = sbuf("cva", [128, 2048])
    halog = sbuf("halog", [128, 128], BF16)
    haloc = sbuf("haloc", [128, 32], BF16)
    phbuf = sbuf("phbuf", [128, 32], BF16)
    Bfm = sbuf("Bfm", [128, 1024], BF16)
    Cfm = sbuf("Cfm", [128, 1024], BF16)
    Btm = sbuf("Btm", [128, 1024], BF16)
    xtmb = sbuf("xtmb", [128, 2048], BF16)
    bcmb = sbuf("bcmb", [128, 256])
    e1s = sbuf("e1s", [128, 256])
    msb = sbuf("msb", [128, 512], BF16)
    xds = sbuf("xds", [128, 256], BF16)
    ces = sbuf("ces", [128, 512], BF16)
    z9s = sbuf("z9s", [128, 72])
    wtk = sbuf("wtk", [128, 32])
    tapb = xin32

    mmA = psum("mmA", [128, 512])
    mmB = psum("mmB", [128, 512])
    sm32 = psum("sm32", [128, 512])
    psT = psum("psT", [128, 1024], BF16)
    psY = psum("psY", [128, 512])
    psS = psum("psS", [128, 512])
    st0 = psum("st0", [128, 512])
    st1 = psum("st1", [128, 512])

    def ctr(name, step=1):
        return Ctr(ctx.enter_context(nc.semaphore(name)), step)

    qS = ctr("qS", 16)
    qG = ctr("qG", 16)
    cPE = ctr("cPE")
    cDV = ctr("cDV")
    cAC = ctr("cAC")
    cPL = ctr("cPL")
    cCC = ctr("cCC")
    ALLC = (qS, qG, cPE, cDV, cAC, cPL, cCC)

    pl = Planner()
    EM = pl.emit

    def barrier():
        vals = tuple((c, c.n) for c in ALLC if c.n > 0)
        for e in Planner.ENGS:
            def run(eng, vals=vals):
                for c, v in vals:
                    eng.wait_ge(c.sem, v)
            pl.ops[e].append(run)

    def tap_dump(name, src, cols):
        if name not in taps:
            return
        barrier()
        for i in range((cols + 1023) // 1024):
            w = min(1024, cols - i * 1024)
            EM("vector", lambda e, i=i, w=w, src=src: e.tensor_copy(
                out=tapb[:, 0:w], in_=src[:, i * 1024:i * 1024 + w]), inc=cDV)
            EM("sync", lambda e, i=i, w=w, name=name: e.dma_start(
                out=taps[name][:, i * 1024:i * 1024 + w], in_=tapb[:, 0:w]),
               wait=[(cDV, cDV.n)], inc=qS)
            barrier()

    for sbt, src in [(Wg9, "Wg9_img"), (idbfs, "id32")]:
        EM("gpsimd", lambda e, d=sbt, s=src: e.dma_start(out=d[:, :], in_=dram[s]),
           inc=qG)
    for sbt, src in [(convw, "convw_img"), (convb, "convb_img"), (dtbc, "dtb_col"),
                     (alogc, "Alog_col"), (dcolc, "D_col"), (nrmw, "normw_img"),
                     (id32s, "id32"), (trius, "triu"), (mco, "mcoef"),
                     (cco, "ccoef"), (bthr, "bthr_col"), (repm, "repmat"),
                     (phal, "phalo"), (bbimg, "bbimg")]:
        EM("sync", lambda e, d=sbt, s=src: e.dma_start(out=d[:, :], in_=dram[s]),
           inc=qS)
    pre_q = [(qS, qS.n), (qG, qG.n)]
    for f in [
              lambda e: e.memset(ones_f[:, :], 1.0),
              lambda e: e.memset(neg128[:, :], -1.0),
              lambda e: e.memset(onescbf[:, :], 1.0),
              lambda e: e.memset(ones128[:, :], 1.0),
              lambda e: e.memset(pzfm[:, 0:1], 0.0),
              lambda e: e.memset(sqbf[:, :], 0.0)]:
        EM("vector", f)
    EM("vector", lambda e: e.memset(haloc[:, :], 0.0), wait=pre_q, inc=cDV)
    EM("vector", lambda e: e.tensor_scalar_mul(halfb[:, :], bbimg[:, :], 0.5),
       wait=pre_q, inc=cDV)
    EM("scalar", lambda e: e.activation(out=acolc[:, :], in_=alogc[:, :],
                                        func=AF.Exp), wait=pre_q, inc=cAC)
    EM("vector", lambda e: e.tensor_scalar_mul(acolc[:, :], acolc[:, :], -1.0),
       wait=[(cAC, cAC.n)], inc=cDV)
    EM("tensor", lambda e: e.transpose(sm32[0:1, 0:128], dcolc[:, :], id32s[:, :]),
       wait=[(qS, qS.n), (cDV, cDV.n)], inc=cPE)
    EM("vector", lambda e: e.tensor_copy(out=srow4[0:1, 0:128],
                                         in_=sm32[0:1, 0:128]),
       wait=[(cPE, cPE.n)], inc=cDV)
    EM("tensor", lambda e: e.matmul(sm32[:, 128:256], ones_f[0:1, 0:128],
                                    srow4[0:1, 0:128], start=True, stop=True),
       wait=[(cDV, cDV.n)], inc=cPE)
    EM("vector", lambda e: e.tensor_copy(out=DB[:, :], in_=sm32[:, 128:256]),
       wait=[(cPE, cPE.n)], inc=cDV)
    barrier()

    def ln_phase(src_name):
        src = dram[src_name]
        loc_cast, loc_sq, loc_mm = {}, {}, {}
        for kc in range(KD):
            sl = kc % 2
            w = [(cPE, loc_mm.get(kc - 2, 0)), (cAC, loc_sq.get(kc - 2, 0))]
            EM("sync", lambda e, kc=kc, sl=sl: e.dma_start(
                out=xin32[:, sl * 1024:sl * 1024 + 1024],
                in_=src[kc * 128:kc * 128 + 128, :]), wait=w, inc=qS)
            ldv = qS.n
            EM("vector", lambda e, sl=sl: e.tensor_copy(
                out=castbf[:, sl * 1024:sl * 1024 + 1024],
                in_=xin32[:, sl * 1024:sl * 1024 + 1024]),
               wait=[(qS, ldv)], inc=cDV)
            loc_cast[kc] = cDV.n
            EM("scalar", lambda e, sl=sl: e.activation(
                out=sqbf[:, 0:1024],
                in_=xin32[:, sl * 1024:sl * 1024 + 1024], func=AF.Square),
               wait=[(qS, ldv), (cPE, loc_mm.get(kc - 1, 0))], inc=cAC)
            loc_sq[kc] = cAC.n
            for tt in range(2):
                stp = st0 if tt == 0 else st1
                EM("tensor", lambda e, kc=kc, sl=sl, tt=tt, stp=stp: e.matmul(
                    stp[0:1, 0:512], onescbf[:, 0:1],
                    castbf[:, sl * 1024 + tt * 512:sl * 1024 + tt * 512 + 512],
                    start=(kc == 0), stop=(kc == KD - 1)),
                   wait=[(cDV, loc_cast[kc])], inc=cPE)
                EM("tensor", lambda e, kc=kc, sl=sl, tt=tt, stp=stp: e.matmul(
                    stp[64:65, 0:512], onescbf[:, 0:1],
                    sqbf[:, tt * 512:tt * 512 + 512],
                    start=(kc == 0), stop=(kc == KD - 1)),
                   wait=[(cAC, loc_sq[kc])], inc=cPE)
            loc_mm[kc] = cPE.n
        pe_done = cPE.n
        for tt in range(2):
            stp = st0 if tt == 0 else st1
            EM("vector", lambda e, tt=tt, stp=stp: e.tensor_copy(
                out=srow4[0:1, tt * 512:tt * 512 + 512], in_=stp[0:1, 0:512]),
               wait=[(cPE, pe_done)], inc=cDV)
            EM("vector", lambda e, tt=tt, stp=stp: e.tensor_copy(
                out=srow4[64:65, tt * 512:tt * 512 + 512], in_=stp[64:65, 0:512]),
               wait=[(cPE, pe_done)], inc=cDV)
        rows_d = cDV.n
        for tt in range(2):
            stp = st0 if tt == 0 else st1
            EM("tensor", lambda e, tt=tt, stp=stp: e.matmul(
                stp[:, 0:512], ones_f[0:1, 0:128],
                srow4[0:1, tt * 512:tt * 512 + 512], start=True, stop=True),
               wait=[(cDV, rows_d)], inc=cPE)
            EM("vector", lambda e, tt=tt, stp=stp: e.tensor_scalar_mul(
                bc_mr[:, tt * 512:tt * 512 + 512], stp[:, 0:512], 1.0 / 4096),
               wait=[(cPE, cPE.n)], inc=cDV)
            EM("tensor", lambda e, tt=tt, stp=stp: e.matmul(
                stp[:, 0:512], ones128[64:65, 0:128],
                srow4[64:65, tt * 512:tt * 512 + 512], start=True, stop=True),
               wait=[(cDV, cDV.n)], inc=cPE)
            EM("vector", lambda e, tt=tt, stp=stp: e.tensor_scalar_mul(
                bc_r[:, tt * 512:tt * 512 + 512], stp[:, 0:512], 1.0 / 4096),
               wait=[(cPE, cPE.n)], inc=cDV)
        bc0 = cDV.n
        EM("vector", lambda e: e.tensor_tensor(
            out=xin32[:, 0:1024], in0=bc_mr[:, :], in1=bc_mr[:, :],
            op=ALU.mult), wait=[(cDV, bc0)], inc=cDV)
        EM("vector", lambda e: e.tensor_tensor(
            out=bc_r[:, :], in0=bc_r[:, :], in1=xin32[:, 0:1024],
            op=ALU.subtract), inc=cDV)
        EM("vector", lambda e: e.tensor_scalar_add(bc_r[:, :], bc_r[:, :], EPS),
           inc=cDV)
        var_d = cDV.n
        EM("scalar", lambda e: e.activation(out=bc_r[:, :], in_=bc_r[:, :],
                                            func=AF.Sqrt),
           wait=[(cDV, var_d)], inc=cAC)
        EM("vector", lambda e: e.reciprocal(bc_r[:, :], bc_r[:, :]),
           wait=[(cAC, cAC.n)], inc=cDV)
        EM("vector", lambda e: e.tensor_tensor(
            out=bc_mr[:, :], in0=bc_mr[:, :], in1=bc_r[:, :], op=ALU.mult),
           inc=cDV)
        bcd = cDV.n
        nrm = {}
        for kc in range(KD):
            sl = kc % 2
            EM("sync", lambda e, kc=kc, sl=sl: e.dma_start(
                out=xin32[:, sl * 1024:sl * 1024 + 1024],
                in_=src[kc * 128:kc * 128 + 128, :]),
               wait=[(cPL, nrm.get(kc - 2, 0)), (cDV, bcd)], inc=qS)
            ldv = qS.n
            EM("vector", lambda e, sl=sl: e.tensor_tensor(
                out=xin32[:, sl * 1024:sl * 1024 + 1024],
                in0=xin32[:, sl * 1024:sl * 1024 + 1024],
                in1=bc_r[:, :], op=ALU.mult), wait=[(qS, ldv), (cDV, bcd)],
               inc=cDV)
            EM("gpsimd", lambda e, kc=kc, sl=sl: e.tensor_tensor(
                out=arena[:, XN0 + kc * 1024:XN0 + kc * 1024 + 1024],
                in0=xin32[:, sl * 1024:sl * 1024 + 1024],
                in1=bc_mr[:, :], op=ALU.subtract), wait=[(cDV, cDV.n)], inc=cPL)
            nrm[kc] = cPL.n
        barrier()

    def ffn_phase(which, parts='all'):
        resid_src = dram["xT"] if which == 1 else dram["park2"]
        dest = dram["park1"] if which == 1 else outT
        EM("gpsimd", lambda e: e.dma_start(out=asl(BC0, BC0 + 4096),
                                           in_=dram["Bcat_img"]), inc=qG)
        EM("gpsimd", lambda e: e.dma_start(out=wWB[:, :], in_=dram["Acat_img"]),
           inc=qG)
        acat_ld = qG.n
        for j in range(8):
            for kc in range(KD):
                EM("tensor", lambda e, j=j, kc=kc: e.matmul(
                    sm32[:, j * 9:j * 9 + 9],
                    arena[:, XN0 + kc * 1024 + j * 128:XN0 + kc * 1024 + j * 128 + 128],
                    Wg9[:, kc * 9:kc * 9 + 9],
                    start=(kc == 0), stop=(kc == KD - 1)), inc=cPE)
        pe_g9 = cPE.n
        EM("vector", lambda e: e.tensor_copy(out=z9s[:, :], in_=sm32[:, 0:72]),
           wait=[(cPE, pe_g9)], inc=cDV)
        z9d = cDV.n
        if parts == 'g9':
            tap_dump("z9", z9s, 72)
            EM("sync", lambda e: e.dma_start(out=dram["park1"], in_=dram["xT"]),
               inc=qS)
            barrier()
            return
        EM("vector", lambda e: e.tensor_scalar(
            wtk[:, 0:8],
            z9s[:, :].rearrange("p (j e) -> p j e", e=9)[:, :, 8],
            bthr[:, 0:1], None, ALU.add), wait=[(cDV, z9d)], inc=cDV)
        EM("scalar", lambda e: e.activation(out=wtk[:, 8:16], in_=wtk[:, 0:8],
                                            func=AF.Sigmoid, scale=1.0),
           wait=[(cDV, cDV.n)], inc=cAC)
        EM("vector", lambda e: e.tensor_scalar_mul(wtk[:, 8:16], wtk[:, 8:16],
                                                   0.125),
           wait=[(cAC, cAC.n)], inc=cDV)
        thr_d = cDV.n
        for j in range(8):
            EM("vector", lambda e, j=j: e.tensor_scalar_min(
                z9s[:, j * 9:j * 9 + 8], z9s[:, j * 9:j * 9 + 8], 30.0),
               wait=[(cDV, thr_d)], inc=cDV)
            EM("scalar", lambda e, j=j: e.activation(
                out=z9s[:, j * 9:j * 9 + 8], in_=z9s[:, j * 9:j * 9 + 8],
                func=AF.Exp), wait=[(cDV, cDV.n)], inc=cAC)
            EM("vector", lambda e, j=j: e.reduce_sum(
                wtk[:, 16 + j:17 + j], z9s[:, j * 9:j * 9 + 8], axis=AX.X),
               wait=[(cAC, cAC.n)], inc=cDV)
            EM("vector", lambda e, j=j: e.reciprocal(
                wtk[:, 24 + j:25 + j], wtk[:, 16 + j:17 + j]),
               wait=[(cDV, cDV.n)], inc=cDV)
            EM("vector", lambda e, j=j: e.tensor_scalar(
                z9s[:, j * 9:j * 9 + 8], z9s[:, j * 9:j * 9 + 8],
                wtk[:, 24 + j:25 + j], None, ALU.mult),
               wait=[(cDV, cDV.n)], inc=cDV)
            EM("vector", lambda e, j=j: e.tensor_scalar(
                z9s[:, j * 9:j * 9 + 8], z9s[:, j * 9:j * 9 + 8],
                wtk[:, 8 + j:9 + j], None, ALU.subtract),
               wait=[(cDV, cDV.n)], inc=cDV)
            EM("vector", lambda e, j=j: e.tensor_relu(
                z9s[:, j * 9:j * 9 + 8], z9s[:, j * 9:j * 9 + 8]),
               wait=[(cDV, cDV.n)], inc=cDV)
            EM("vector", lambda e, j=j: e.reduce_sum(
                wtk[:, 16 + j:17 + j], z9s[:, j * 9:j * 9 + 8], axis=AX.X),
               wait=[(cDV, cDV.n)], inc=cDV)
            EM("vector", lambda e, j=j: e.tensor_scalar_max(
                wtk[:, 16 + j:17 + j], wtk[:, 16 + j:17 + j], 1e-30),
               wait=[(cDV, cDV.n)], inc=cDV)
            EM("vector", lambda e, j=j: e.reciprocal(
                wtk[:, 24 + j:25 + j], wtk[:, 16 + j:17 + j]),
               wait=[(cDV, cDV.n)], inc=cDV)
            EM("vector", lambda e, j=j: e.tensor_scalar(
                z9s[:, j * 9:j * 9 + 8], z9s[:, j * 9:j * 9 + 8],
                wtk[:, 24 + j:25 + j], 2.0, ALU.mult, ALU.mult),
               wait=[(cDV, cDV.n)], inc=cDV)
        wd = cDV.n
        if which == 1:
            tap_dump("z9", z9s, 72)
        if parts == 'w':
            EM("sync", lambda e: e.dma_start(out=dram["park1"], in_=dram["xT"]),
               inc=qS)
            barrier()
            return
        for j in range(8):
            bk = sm32 if j < 4 else st0
            EM("tensor", lambda e, j=j, bk=bk: e.transpose(
                bk[0:8, (j % 4) * 128:(j % 4) * 128 + 128],
                z9s[:, j * 9:j * 9 + 8], id32s[:, :]),
               wait=[(cDV, wd)], inc=cPE)
        wtr = cPE.n
        for j in range(8):
            bk = sm32 if j < 4 else st0
            EM("vector", lambda e, j=j, bk=bk: e.tensor_copy(
                out=bc_r[0:8, j * 128:j * 128 + 128],
                in_=bk[0:8, (j % 4) * 128:(j % 4) * 128 + 128]),
               wait=[(cPE, wtr)], inc=cDV)
        wrd = cDV.n
        if parts == 'wt':
            EM("sync", lambda e: e.dma_start(out=dram["park1"], in_=dram["xT"]),
               inc=qS)
            barrier()
            return
        for tt in range(2):
            stp = mmA if tt == 0 else mmB
            for kc in range(KD):
                EM("tensor", lambda e, tt=tt, kc=kc, stp=stp: e.matmul(
                    stp[:, 0:512], wWB[:, kc * 128:kc * 128 + 128],
                    arena[:, XN0 + kc * 1024 + tt * 512:XN0 + kc * 1024 + tt * 512 + 512],
                    start=(kc == 0), stop=(kc == KD - 1)),
                   wait=([(qG, acat_ld), (cDV, wrd)] if kc == 0 else []), inc=cPE)
            hmm = cPE.n
            if parts == 'h1':
                continue
            EM("tensor", lambda e, tt=tt: e.matmul(
                sm32[:, 0:512], repm[0:8, :],
                bc_r[0:8, tt * 512:tt * 512 + 512], start=True, stop=True),
               wait=[(cDV, wrd), (cPE, hmm), (cDV, cDV.n)], inc=cPE)
            wbd = cPE.n
            EM("vector", lambda e, tt=tt, stp=stp: e.tensor_copy(
                out=Bfm[:, tt * 512:tt * 512 + 512], in_=stp[:, 0:512]),
               wait=[(cPE, wbd)], inc=cDV)
            EM("vector", lambda e, tt=tt: e.tensor_tensor(
                out=Bfm[:, tt * 512:tt * 512 + 512],
                in0=Bfm[:, tt * 512:tt * 512 + 512],
                in1=sm32[:, 0:512], op=ALU.mult), wait=[(cDV, cDV.n)], inc=cDV)
        hwd = cDV.n
        if parts == 'h1':
            EM("sync", lambda e: e.dma_start(out=dram["park1"], in_=dram["xT"]),
               inc=qS)
            barrier()
            return
        if which == 1:
            tap_dump("hw", Bfm, 1024)
        if parts == 'h':
            EM("sync", lambda e: e.dma_start(out=dram["park1"], in_=dram["xT"]),
               inc=qS)
            barrier()
            return
        if parts == 'smalls':
            EM("sync", lambda e: e.dma_start(out=dram["park1"], in_=dram["xT"]),
               inc=qS)
            barrier()
            return
        slab_mm = {}
        ep_st = {}
        for oc in range(KD):
            wb = wWA
            wb = wWA if oc % 2 == 0 else wWB
            wwait = [(cPE, slab_mm.get(oc - 2, 0))]
            if oc % 2 == 1 or oc == 0:
                pass
            if oc <= 1:
                wwait.append((cDV, hwd))
            EM("gpsimd", lambda e, oc=oc, wb=wb: e.dma_start(
                out=wb[:, :],
                in_=dram["Wb_t"][oc, :, :, :].rearrange("d k o -> d (k o)")),
               wait=wwait + ([(cPE, cPE.n)] if oc == 1 else []), inc=qG)
            wld = qG.n
            for tt in range(2):
                pp = [mmA, mmB, psY, psS][(oc % 2) * 2 + tt]
                for kc in range(KD):
                    EM("tensor", lambda e, oc=oc, tt=tt, kc=kc, pp=pp, wb=wb:
                       e.matmul(pp[:, 0:512], wb[:, kc * 128:kc * 128 + 128],
                                arena[:, XN0 + kc * 1024 + tt * 512:XN0 + kc * 1024 + tt * 512 + 512],
                                start=(kc == 0), stop=False),
                       wait=([(qG, wld), (cDV, ep_st.get((oc - 2, tt), hwd))]
                             if kc == 0 else []), inc=cPE)
                EM("tensor", lambda e, oc=oc, tt=tt, pp=pp: e.matmul(
                    pp[:, 0:512], asl(BC0, BC0 + 4096)[:, oc * 128:oc * 128 + 128],
                    Bfm[:, tt * 512:tt * 512 + 512], start=False, stop=True),
                   wait=[(cDV, hwd)] if oc == 0 else [], inc=cPE)
                mmdone = cPE.n
                slab_mm[oc] = mmdone
                sl = (oc * 2 + tt) % 4
                EM("sync", lambda e, oc=oc, tt=tt, sl=sl: e.dma_start(
                    out=xin32[:, sl * 512:sl * 512 + 512],
                    in_=resid_src[oc * 128:oc * 128 + 128, tt * 512:tt * 512 + 512]),
                   wait=[(cDV, ep_st.get((oc - 2, tt), 0))], inc=qS)
                rld = qS.n
                EM("vector", lambda e, oc=oc, tt=tt, pp=pp, sl=sl: e.tensor_scalar(
                    cva[:, tt * 1024:tt * 1024 + 512], pp[:, 0:512], 0.5,
                    halfb[:, oc:oc + 1], ALU.mult, ALU.add),
                   wait=[(cPE, mmdone)], inc=cDV)
                EM("vector", lambda e, oc=oc, tt=tt, sl=sl: e.tensor_tensor(
                    out=cva[:, tt * 1024:tt * 1024 + 512],
                    in0=cva[:, tt * 1024:tt * 1024 + 512],
                    in1=xin32[:, sl * 512:sl * 512 + 512], op=ALU.add),
                   wait=[(qS, rld)], inc=cDV)
                evd = cDV.n
                ep_st[(oc, tt)] = evd
                EM("sync", lambda e, oc=oc, tt=tt, dest=dest: e.dma_start(
                    out=dest[oc * 128:oc * 128 + 128, tt * 512:tt * 512 + 512],
                    in_=cva[:, tt * 1024:tt * 1024 + 512]),
                   wait=[(cDV, evd)], inc=qS)
        barrier()

    def dt_phase():
        EM("gpsimd", lambda e: e.dma_start(out=wWA[:, :], in_=dram["Wdt_img"]),
           inc=qG)
        wld = qG.n
        for tt in range(2):
            pp = mmA if tt == 0 else mmB
            for kc in range(KD):
                EM("tensor", lambda e, tt=tt, kc=kc, pp=pp: e.matmul(
                    pp[:, 0:512], wWA[:, kc * 128:kc * 128 + 128],
                    arena[:, XN0 + kc * 1024 + tt * 512:XN0 + kc * 1024 + tt * 512 + 512],
                    start=(kc == 0), stop=(kc == KD - 1)),
                   wait=[(qG, wld)] if kc == 0 else [], inc=cPE)
            EM("vector", lambda e, tt=tt, pp=pp: e.tensor_copy(
                out=bc_mr[:, tt * 512:tt * 512 + 512], in_=pp[:, 0:512]),
               wait=[(cPE, cPE.n)], inc=cDV)
        EM("vector", lambda e: e.tensor_scalar(
            bc_mr[:, :], bc_mr[:, :], dtbc[:, 0:1], None, ALU.add),
           wait=[(cDV, cDV.n)], inc=cDV)
        a_d = cDV.n
        EM("scalar", lambda e: e.activation(out=dts[:, :], in_=bc_mr[:, :],
                                            func=AF.Abs),
           wait=[(cDV, a_d)], inc=cAC)
        EM("scalar", lambda e: e.activation(out=dts[:, :], in_=dts[:, :],
                                            func=AF.Exp, scale=-1.0), inc=cAC)
        EM("vector", lambda e: e.tensor_scalar_add(dts[:, :], dts[:, :], 1.0),
           wait=[(cAC, cAC.n)], inc=cDV)
        EM("scalar", lambda e: e.activation(out=dts[:, :], in_=dts[:, :],
                                            func=AF.Ln),
           wait=[(cDV, cDV.n)], inc=cAC)
        EM("vector", lambda e: e.tensor_scalar_max(bc_mr[:, :], bc_mr[:, :], 0.0),
           wait=[(cAC, cAC.n)], inc=cDV)
        EM("vector", lambda e: e.tensor_tensor(
            out=dts[:, :], in0=dts[:, :], in1=bc_mr[:, :], op=ALU.add),
           wait=[(cDV, cDV.n)], inc=cDV)
        EM("vector", lambda e: e.tensor_scalar_min(dts[:, :], dts[:, :], 100.0),
           wait=[(cDV, cDV.n)], inc=cDV)
        dt_d = cDV.n
        tap_dump("dt", dts, T)
        EM("vector", lambda e: e.tensor_scalar(
            acum[:, :], dts[:, :], acolc[:, 0:1], None, ALU.mult),
           wait=[(cDV, dt_d)], inc=cDV)
        ad_d = cDV.n
        for z in range(NZ):
            EM("vector", lambda e, z=z: e.tensor_tensor_scan(
                acum[:, z * 128:z * 128 + 128], ones128[:, :],
                acum[:, z * 128:z * 128 + 128], 0.0, ALU.mult, ALU.add),
               wait=[(cDV, ad_d)], inc=cDV)
        ac_d = cDV.n
        tap_dump("acum", acum, T)
        for z in range(NZ):
            EM("vector", lambda e, z=z: e.tensor_scalar(
                bc_mr[:, z * 128:z * 128 + 128], acum[:, z * 128:z * 128 + 128],
                acum[:, z * 128 + 127:z * 128 + 128], None, ALU.subtract),
               wait=[(cDV, ac_d)], inc=cDV)
            EM("scalar", lambda e, z=z: e.activation(
                out=bc_mr[:, z * 128:z * 128 + 128],
                in_=bc_mr[:, z * 128:z * 128 + 128], func=AF.Exp, scale=-1.0),
               wait=[(cDV, cDV.n)], inc=cAC)
        dec_d = cAC.n
        EM("vector", lambda e: e.tensor_copy(
            out=chsum[:, :],
            in_=acum[:, :].rearrange("p (z l) -> p z l", l=128)[:, :, 127]),
           wait=[(cDV, ac_d)], inc=cDV)
        ch_d = cDV.n
        EM("vector", lambda e: e.reduce_sum(lpcol[:, :], chsum[:, :], axis=AX.X),
           wait=[(cDV, ch_d)], inc=cDV)
        lp_d = cDV.n
        EM("sync", lambda e: e.dma_start(out=dram["pubLP"], in_=lpcol[:, :]),
           wait=[(cDV, lp_d)], inc=qS)
        EM("gpsimd", lambda e: e.collective_compute(
            "AllGather", ALU.bypass,
            replica_groups=[[0, 1, 2, 3], [4, 5, 6, 7]],
            ins=[dram["pubLP"]], outs=[dram["gathLP"]]),
           wait=[(qS, qS.n)], inc=cCC)
        EM("sync", lambda e: e.dma_start(
            out=lps[:, :],
            in_=dram["gathLP"].rearrange("(j p) o -> p (j o)", p=128),
            allow_slow_non_contiguous=True),
           wait=[(cCC, cCC.n)], inc=qS)
        lps_d = qS.n
        EM("scalar", lambda e: e.activation(out=tzfm[:, :], in_=chsum[:, :],
                                            func=AF.Exp),
           wait=[(cDV, ch_d)], inc=cAC)
        tz_d = cAC.n
        EM("vector", lambda e: e.tensor_tensor_scan(
            pzfm[:, 1:9], ones128[:, 0:8], chsum[:, :], 0.0, ALU.mult, ALU.add),
           wait=[(cDV, ch_d)], inc=cDV)
        pz_d = cDV.n
        EM("scalar", lambda e: e.activation(out=pzfm[:, 0:8], in_=pzfm[:, 0:8],
                                            func=AF.Exp),
           wait=[(cDV, pz_d)], inc=cAC)
        pz_e = cAC.n
        ev_round = 0
        for rnd in range(2):
            zs = range(rnd * 4, rnd * 4 + 4)
            for z in zs:
                EM("tensor", lambda e, z=z: e.transpose(
                    sm32[:, (z % 4) * 128:(z % 4) * 128 + 128],
                    dts[:, z * 128:z * 128 + 128], id32s[:, :]),
                   wait=[(cDV, dt_d), (cAC, dec_d), (cDV, ev_round)], inc=cPE)
                EM("tensor", lambda e, z=z: e.transpose(
                    st0[:, (z % 4) * 128:(z % 4) * 128 + 128],
                    bc_mr[:, z * 128:z * 128 + 128], id32s[:, :]),
                   wait=[(cDV, ev_round)], inc=cPE)
            trd = cPE.n
            for z in zs:
                EM("vector", lambda e, z=z: e.tensor_copy(
                    out=dtT[:, z * 128:z * 128 + 128],
                    in_=sm32[:, (z % 4) * 128:(z % 4) * 128 + 128]),
                   wait=[(cPE, trd)], inc=cDV)
                EM("vector", lambda e, z=z: e.tensor_copy(
                    out=decT[:, z * 128:z * 128 + 128],
                    in_=st0[:, (z % 4) * 128:(z % 4) * 128 + 128]),
                   wait=[(cPE, trd)], inc=cDV)
            ev_round = cDV.n
        tap_dump("dtT", dtT, 1024)
        def bcast_zh2(src_sb, ncol, dst_sb, waits):
            EM("tensor", lambda e, src_sb=src_sb, ncol=ncol: e.transpose(
                sm32[0:ncol, 0:128], src_sb[:, 0:ncol], id32s[:, :]),
               wait=waits + [(cDV, cDV.n), (cPE, cPE.n)], inc=cPE)
            EM("vector", lambda e: e.tensor_copy(
                out=srow4[0:8, 0:128], in_=sm32[0:8, 0:128]),
               wait=[(cPE, cPE.n)], inc=cDV)
            EM("sync", lambda e: e.dma_start(out=paf[0:1, 0:1024],
                                             in_=srow4[0:8, 0:128]),
               wait=[(cDV, cDV.n)], inc=qS)
            fl = qS.n
            for tt in range(2):
                stp = st0 if tt == 0 else st1
                EM("tensor", lambda e, tt=tt, stp=stp: e.matmul(
                    stp[:, 0:512], ones_f[0:1, 0:128],
                    paf[0:1, tt * 512:tt * 512 + 512], start=True, stop=True),
                   wait=[(qS, fl), (cDV, cDV.n)], inc=cPE)
                EM("vector", lambda e, tt=tt, stp=stp: e.tensor_copy(
                    out=dst_sb[:, tt * 512:tt * 512 + 512], in_=stp[:, 0:512]),
                   wait=[(cPE, cPE.n)], inc=cDV)
        bcast_zh2(tzfm, 8, TB, [(cAC, tz_d)])
        bcast_zh2(pzfm, 8, PB, [(cAC, pz_e)])
        tap_dump("TB", TB, 1024)
        tap_dump("PB", PB, 1024)
        EM("vector", lambda e: e.memset(wlog[:, :], 0.0),
           wait=[(qS, lps_d)], inc=cDV)
        for j in range(4):
            for i in range(4):
                EM("vector", lambda e, j=j, i=i: e.scalar_tensor_tensor(
                    out=wlog[:, j:j + 1], in0=lps[:, i:i + 1],
                    scalar=cco[:, j * 4 + i:j * 4 + i + 1],
                    in1=wlog[:, j:j + 1], op0=ALU.mult, op1=ALU.add),
                   wait=[(cDV, cDV.n)], inc=cDV)
        EM("scalar", lambda e: e.activation(out=wlog[:, :], in_=wlog[:, :],
                                            func=AF.Exp),
           wait=[(cDV, cDV.n)], inc=cAC)
        EM("vector", lambda e: e.tensor_tensor(out=wlog[:, :], in0=wlog[:, :],
                                               in1=mco[:, :], op=ALU.mult),
           wait=[(cAC, cAC.n)], inc=cDV)
        wl_d = cDV.n
        EM("tensor", lambda e: e.transpose(sm32[0:4, 0:128], wlog[:, :],
                                           id32s[:, :]),
           wait=[(cDV, wl_d), (cPE, cPE.n)], inc=cPE)
        EM("vector", lambda e: e.tensor_copy(out=srow4[0:4, 0:128],
                                             in_=sm32[0:4, 0:128]),
           wait=[(cPE, cPE.n)], inc=cDV)
        EM("sync", lambda e: e.dma_start(out=paf[0:1, 0:512],
                                         in_=srow4[0:4, 0:128]),
           wait=[(cDV, cDV.n)], inc=qS)
        fl = qS.n
        EM("tensor", lambda e: e.matmul(st0[:, 0:512], ones_f[0:1, 0:128],
                                        paf[0:1, 0:512], start=True, stop=True),
           wait=[(qS, fl), (cDV, cDV.n)], inc=cPE)
        EM("vector", lambda e: e.tensor_copy(out=WC[:, :], in_=st0[:, 0:512]),
           wait=[(cPE, cPE.n)], inc=cDV)
        tap_dump("WC", WC, 512)
        EM("sync", lambda e: e.dma_start(out=dram["park2"], in_=dram["park1"]),
           inc=qS)
        barrier()

    def mamba_group(g, parts='all'):
        hl0 = g * 16
        gp_st = {}
        oc_order = list(range(8, 18)) + list(range(0, 8))
        slab_mm = {}
        evac_d = {}
        phl_last = 0
        halog_q = 0
        for idx, oc in enumerate(oc_order):
            wb = wWA if idx % 2 == 0 else wWB
            EM("gpsimd", lambda e, g=g, oc=oc, wb=wb: e.dma_start(
                out=wb[:, :],
                in_=dram["Win_t"][g, oc, :, :, :].rearrange("d k o -> d (k o)")),
               wait=[(cPE, slab_mm.get(idx - 2, 0))], inc=qG)
            wld = qG.n
            for tt in range(2):
                pp = [mmA, mmB, psY, psS][(idx % 2) * 2 + tt]
                for kc in range(KD):
                    EM("tensor", lambda e, tt=tt, kc=kc, pp=pp, wb=wb: e.matmul(
                        pp[:, 0:512], wb[:, kc * 128:kc * 128 + 128],
                        arena[:, XN0 + kc * 1024 + tt * 512:XN0 + kc * 1024 + tt * 512 + 512],
                        start=(kc == 0), stop=(kc == KD - 1)),
                       wait=([(qG, wld), (cDV, evac_d.get((idx - 2, tt), 0))]
                             if kc == 0 else []), inc=cPE)
                mmdone = cPE.n
                slab_mm[idx] = mmdone
                if oc >= 8:
                    cc = oc - 8
                    if cc < 8:
                        dst = arena[:, HS0 + cc * 1024 + tt * 512:HS0 + cc * 1024 + tt * 512 + 512]
                    elif cc == 8:
                        dst = Bfm[:, tt * 512:tt * 512 + 512]
                    else:
                        dst = Cfm[:, tt * 512:tt * 512 + 512]
                    EM("vector", lambda e, dst=dst, pp=pp: e.tensor_copy(
                        out=dst, in_=pp[:, 0:512]),
                       wait=[(cPE, mmdone)], inc=cDV)
                else:
                    sl = oc % 2
                    EM("vector", lambda e, tt=tt, sl=sl, pp=pp: e.tensor_copy(
                        out=castbf[:, sl * 1024 + tt * 512:sl * 1024 + tt * 512 + 512],
                        in_=pp[:, 0:512]),
                       wait=[(cPE, mmdone), (qS, gp_st.get(oc - 2, 0))], inc=cDV)
                evac_d[(idx, tt)] = cDV.n
            if oc >= 8:
                cc = oc - 8
                if cc < 8:
                    src3 = arena[:, HS0 + cc * 1024 + 1021:HS0 + cc * 1024 + 1024]
                elif cc == 8:
                    src3 = Bfm[:, 1021:1024]
                else:
                    src3 = Cfm[:, 1021:1024]
                EM("vector", lambda e, cc=cc, src3=src3: e.tensor_copy(
                    out=phbuf[:, cc * 3:cc * 3 + 3], in_=src3),
                   wait=[(cDV, evac_d[(idx, 1)])], inc=cDV)
                phl_last = cDV.n
            else:
                EM("sync", lambda e, g=g, oc=oc, sl=oc % 2: e.dma_start(
                    out=dram["gatepark"][g * 1024 + oc * 128:g * 1024 + oc * 128 + 128, :],
                    in_=castbf[:, sl * 1024:sl * 1024 + 1024]),
                   wait=[(cDV, evac_d[(idx, 1)])], inc=qS)
                gp_st[oc] = qS.n
            if oc == 17:
                EM("sync", lambda e, g=g: e.dma_start(
                    out=dram[f"pubH{g}"], in_=phbuf[:, :]),
                   wait=[(cDV, phl_last)], inc=qS)
                EM("gpsimd", lambda e, g=g: e.collective_compute(
                    "AllGather", ALU.bypass,
                    replica_groups=[[0, 1, 2, 3], [4, 5, 6, 7]],
                    ins=[dram[f"pubH{g}"]], outs=[dram[f"gatH{g}"]]),
                   wait=[(qS, qS.n)], inc=cCC)
                EM("sync", lambda e, g=g: e.dma_start(
                    out=halog[:, :],
                    in_=dram[f"gatH{g}"].rearrange("(r p) w -> p r w", p=128)),
                   wait=[(cCC, cCC.n)], inc=qS)
                halog_q = qS.n
        if parts == 'inproj':
            barrier()
            return
        EM("vector", lambda e: e.tensor_scalar(
            haloc[:, :], halog[:, 0:32], phal[:, 0:1], None, ALU.mult),
           wait=[(qS, halog_q)], inc=cDV)
        for r in range(1, 4):
            EM("vector", lambda e, r=r: e.scalar_tensor_tensor(
                out=haloc[:, :], in0=halog[:, r * 32:r * 32 + 32],
                scalar=phal[:, r:r + 1], in1=haloc[:, :],
                op0=ALU.mult, op1=ALU.add), wait=[(cDV, cDV.n)], inc=cDV)
        haloc_d = cDV.n
        conv_ac = {}
        cva_used = {"vector": 0, "gpsimd": 0}
        for cc in range(10):
            sl = cc % 2
            gc = g * 10 + cc
            eng = "vector"
            ceng = cDV
            if cc < 8:
                src = arena[:, HS0 + cc * 1024:HS0 + cc * 1024 + 1024]
                dst = arena[:, HS0 + cc * 1024:HS0 + cc * 1024 + 1024]
            elif cc == 8:
                src = Bfm[:, :]
                dst = Bfm[:, :]
            else:
                src = Cfm[:, :]
                dst = Cfm[:, :]
            w0 = [(cDV, evac_d[(cc, 1)]), (cDV, phl_last),
                  (cAC, conv_ac.get(cc - 2, 0))]
            EM(eng, lambda e, sl=sl, gc=gc, src=src: e.tensor_scalar(
                cva[:, sl * 1024:sl * 1024 + 1024], src,
                convw[:, gc * 4 + 3:gc * 4 + 4], None, ALU.mult),
               wait=w0, inc=ceng)
            for k in (2, 1, 0):
                o0 = 3 - k
                EM(eng, lambda e, sl=sl, gc=gc, k=k, o0=o0, src=src:
                   e.scalar_tensor_tensor(
                       out=cva[:, sl * 1024 + o0:sl * 1024 + 1024],
                       in0=src[:, 0:1024 - o0],
                       scalar=convw[:, gc * 4 + k:gc * 4 + k + 1],
                       in1=cva[:, sl * 1024 + o0:sl * 1024 + 1024],
                       op0=ALU.mult, op1=ALU.add), inc=ceng)
            EM(eng, lambda e, sl=sl, gc=gc: e.tensor_scalar(
                cva[:, sl * 1024:sl * 1024 + 1024],
                cva[:, sl * 1024:sl * 1024 + 1024],
                convb[:, gc:gc + 1], None, ALU.add), inc=ceng)
            EM(eng, lambda e: e.nop(), wait=[(cDV, haloc_d)])
            for t in range(3):
                for k in range(3 - t):
                    EM(eng, lambda e, sl=sl, gc=gc, cc=cc, t=t, k=k:
                       e.scalar_tensor_tensor(
                           out=cva[:, sl * 1024 + t:sl * 1024 + t + 1],
                           in0=haloc[:, cc * 3 + t + k:cc * 3 + t + k + 1],
                           scalar=convw[:, gc * 4 + k:gc * 4 + k + 1],
                           in1=cva[:, sl * 1024 + t:sl * 1024 + t + 1],
                           op0=ALU.mult, op1=ALU.add), inc=ceng)
                    EM(eng, lambda e: e.nop(), wait=[(ceng, ceng.n)])
            chain_d = ceng.n
            EM("scalar", lambda e, sl=sl, dst=dst: e.activation(
                out=dst, in_=cva[:, sl * 1024:sl * 1024 + 1024], func=AF.Silu),
               wait=[(ceng, chain_d)], inc=cAC)
            conv_ac[cc] = cAC.n
        conv_done = cAC.n
        if g == 0:
            tap_dump("hs0", arena[:, HS0:HS0 + 8192], 8192)
            tap_dump("b0", Bfm, 1024)
            tap_dump("c0", Cfm, 1024)
        if parts == 'conv':
            barrier()
            return
        EM("vector", lambda e: e.memset(bc_r[:, :], 0.0), inc=cDV)
        scar_init = cDV.n
        ph0_dv = cDV.n
        for z in range(NZ):
            EM("tensor", lambda e, z=z: e.transpose(
                psT[:, z * 128:z * 128 + 128], Bfm[:, z * 128:z * 128 + 128],
                idbfs[:, :]),
               wait=[(cAC, conv_ac[8]), (cDV, ph0_dv)], inc=cPE)
        btr = cPE.n
        for z in range(NZ):
            EM("vector", lambda e, z=z: e.tensor_copy(
                out=Btm[:, z * 128:z * 128 + 128],
                in_=psT[:, z * 128:z * 128 + 128]),
               wait=[(cPE, btr)], inc=cDV)
        btm_d = cDV.n
        BANKS = [sm32, st0, psY, psS]
        d_mm = {}
        dexp = {}
        m_dd = {}
        ydmm = {}
        stmm = {}
        scd = {}
        ydev = {}
        xd_dd = {}
        paf_used = {}
        bcm_used = {}
        zpe = {}
        zdv = {}
        xtm_ev = {}
        for z in range(NZ):
            zp = z % 2
            EM("sync", lambda e, g=g, z=z: e.dma_start(
                out=paf[0:1, 0:1024],
                in_=acum[g * 16:g * 16 + 8, z * 128:z * 128 + 128]),
               wait=[(cPE, paf_used.get(z - 1, 0))], inc=qS)
            EM("sync", lambda e, g=g, z=z: e.dma_start(
                out=paf[64:65, 0:1024],
                in_=acum[g * 16 + 8:g * 16 + 16, z * 128:z * 128 + 128]),
               inc=qS)
            paf_d = qS.n
            bmk = mmA if zp == 0 else mmB
            EM("tensor", lambda e, z=z, bmk=bmk: e.matmul(
                bmk[:, 0:128],
                Bfm[:, z * 128:z * 128 + 128], Cfm[:, z * 128:z * 128 + 128],
                start=True, stop=True),
               wait=[(cAC, conv_ac[9]), (cDV, bcm_used.get(z - 2, ph0_dv))],
               inc=cPE)
            EM("vector", lambda e, zp=zp, bmk=bmk: e.tensor_tensor(
                out=bcmb[:, zp * 128:zp * 128 + 128],
                in0=bmk[:, 0:128],
                in1=trius[:, :], op=ALU.mult), wait=[(cPE, cPE.n)], inc=cDV)
            bcm_d = cDV.n
            bcm_used[z] = bcm_d
            for hc in range(8):
                EM("tensor", lambda e, z=z, hc=hc: e.transpose(
                    psT[:, hc * 128:hc * 128 + 128],
                    arena[:, HS0 + hc * 1024 + z * 128:HS0 + hc * 1024 + z * 128 + 128],
                    idbfs[:, :]),
                   wait=[(cAC, conv_ac[hc]),
                         (cDV, xtm_ev.get(z - 1, btm_d))], inc=cPE)
            trd = cPE.n
            for hc in range(8):
                for hf in range(2):
                    hl = hc * 2 + hf
                    EM("vector", lambda e, z=z, zp=zp, hc=hc, hf=hf, hl=hl:
                       e.tensor_scalar(
                           xtmb[:, zp * 1024 + hc * 128 + hf * 64:zp * 1024 + hc * 128 + hf * 64 + 64],
                           psT[:, hc * 128 + hf * 64:hc * 128 + hf * 64 + 64],
                           dtT[:, z * 128 + hl0 + hl:z * 128 + hl0 + hl + 1],
                           None, ALU.mult),
                       wait=[(cPE, trd), (cPE, zpe.get(z - 2, 0))], inc=cDV)
            xtm_ev[z] = cDV.n
            xtm_d = cDV.n
            LAG = 2
            def stageA(hl):
                li = z * 16 + hl
                s2 = hl % 2
                bk = BANKS[li % 4]
                pr = (hl // 8) * 64
                EM("tensor", lambda e, hl=hl, bk=bk, pr=pr: e.matmul(
                    bk[:, 0:128],
                    paf[pr:pr + 1, (hl % 8) * 128:(hl % 8) * 128 + 128],
                    ones128[pr:pr + 1, 0:128], start=True, stop=False),
                   wait=[(qS, paf_d), (cAC, dexp.get(li - 4, 0)),
                         (cDV, ydev.get(li - 4, ph0_dv))], inc=cPE)
                EM("tensor", lambda e, hl=hl, bk=bk, pr=pr: e.matmul(
                    bk[:, 0:128], neg128[pr:pr + 1, 0:128],
                    paf[pr:pr + 1, (hl % 8) * 128:(hl % 8) * 128 + 128],
                    start=False, stop=True), inc=cPE)
                dmm = cPE.n
                EM("scalar", lambda e, s2=s2, bk=bk: e.activation(
                    out=e1s[:, s2 * 128:s2 * 128 + 128],
                    in_=bk[:, 0:128], func=AF.Relu),
                   wait=[(cPE, dmm)], inc=cAC)
                EM("scalar", lambda e, s2=s2: e.activation(
                    out=e1s[:, s2 * 128:s2 * 128 + 128],
                    in_=e1s[:, s2 * 128:s2 * 128 + 128], func=AF.Exp,
                    scale=-1.0), wait=[(cPL, m_dd.get(li - 2, 0))], inc=cAC)
                dexp[li] = cAC.n
                EM("gpsimd", lambda e, s4=hl % 4, s2=s2, zp=zp: e.tensor_tensor(
                    out=msb[:, (hl % 4) * 128:(hl % 4) * 128 + 128],
                    in0=e1s[:, s2 * 128:s2 * 128 + 128],
                    in1=bcmb[:, zp * 128:zp * 128 + 128], op=ALU.mult),
                   wait=[(cAC, dexp[li]), (cDV, bcm_d),
                         (cPE, ydmm.get(li - 4, 0))], inc=cPL)
                m_dd[li] = cPL.n
                EM("vector", lambda e, zp=zp, hl=hl, z=z: e.tensor_scalar(
                    xds[:, (hl % 4) * 64:(hl % 4) * 64 + 64],
                    xtmb[:, zp * 1024 + (hl // 2) * 128 + (hl % 2) * 64:
                         zp * 1024 + (hl // 2) * 128 + (hl % 2) * 64 + 64],
                    decT[:, z * 128 + hl0 + hl:z * 128 + hl0 + hl + 1],
                    None, ALU.mult),
                   wait=[(cPE, stmm.get(li - 4, 0))], inc=cDV)
                xd_dd[li] = cDV.n

            def stageB(hl):
                li = z * 16 + hl
                h = hl0 + hl
                s4 = hl % 4
                pb = (hl % 2) * 64
                bk = BANKS[li % 4]
                EM("tensor", lambda e, zp=zp, hl=hl, s4=s4, pb=pb, bk=bk:
                   e.matmul(
                       bk[pb:pb + 64, 128:256],
                       xtmb[:, zp * 1024 + (hl // 2) * 128 + (hl % 2) * 64:
                            zp * 1024 + (hl // 2) * 128 + (hl % 2) * 64 + 64],
                       msb[:, s4 * 128:s4 * 128 + 128],
                       start=True, stop=True),
                   wait=[(cPL, m_dd[li])], inc=cPE)
                ydmm[li] = cPE.n
                EM("tensor", lambda e, z=z, s4=s4, bk=bk: e.matmul(
                    bk[:, 256:320],
                    Btm[:, z * 128:z * 128 + 128],
                    xds[:, s4 * 64:s4 * 64 + 64], start=True, stop=True),
                   wait=[(cDV, xd_dd[li]), (cDV, btm_d)], inc=cPE)
                stmm[li] = cPE.n
                EM("vector", lambda e, z=z, hl=hl: e.tensor_copy(
                    out=arena[:, SH0 + (z * 16 + hl) * 64:SH0 + (z * 16 + hl) * 64 + 64],
                    in_=bc_r[:, hl * 64:hl * 64 + 64]),
                   wait=[(cDV, scar_init)], inc=cDV)
                EM("vector", lambda e, hl=hl, z=z, h=h, bk=bk:
                   e.scalar_tensor_tensor(
                       out=bc_r[:, hl * 64:hl * 64 + 64],
                       in0=bc_r[:, hl * 64:hl * 64 + 64],
                       scalar=TB[:, z * 128 + h:z * 128 + h + 1],
                       in1=bk[:, 256:320],
                       op0=ALU.mult, op1=ALU.add),
                   wait=[(cPE, stmm[li])], inc=cDV)
                scd[li] = cDV.n
                EM("vector", lambda e, z=z, hl=hl, pb=pb, h=h, bk=bk:
                   e.scalar_tensor_tensor(
                       out=arena[pb:pb + 64, HS0 + (hl // 2) * 1024 + z * 128:HS0 + (hl // 2) * 1024 + z * 128 + 128],
                       in0=arena[pb:pb + 64, HS0 + (hl // 2) * 1024 + z * 128:HS0 + (hl // 2) * 1024 + z * 128 + 128],
                       scalar=DB[pb:pb + 64, h:h + 1],
                       in1=bk[pb:pb + 64, 128:256],
                       op0=ALU.mult, op1=ALU.add),
                   wait=[(cPE, stmm[li])], inc=cDV)
                ydev[li] = cDV.n

            for i in range(16 + LAG):
                if i < 16:
                    stageA(i)
                if i >= LAG:
                    stageB(i - LAG)
            paf_used[z] = cPE.n
            zpe[z] = cPE.n
        ph1_dv = cDV.n
        if g == 0:
            tap_dump("sh0", arena[:, SH0:SH0 + 8192], 8192)
        if parts == 'ph1':
            barrier()
            return
        EM("gpsimd", lambda e, g=g: e.dma_start(out=dram[f"pubF{g}"],
                                                in_=bc_r[:, :]),
           wait=[(cDV, ph1_dv)], inc=qG)
        EM("gpsimd", lambda e, g=g: e.collective_compute(
            "AllGather", ALU.bypass,
            replica_groups=[[0, 1, 2, 3], [4, 5, 6, 7]],
            ins=[dram[f"pubF{g}"]], outs=[dram[f"gatF{g}"]]),
           wait=[(qG, qG.n)], inc=cCC)
        cc_f = cCC.n
        EM("vector", lambda e: e.memset(bc_mr[:, :], 0.0), inc=cDV)
        fs_used = {}
        for j in range(4):
            sl = j % 2
            EM("sync", lambda e, g=g, j=j, sl=sl: e.dma_start(
                out=xtmb[:, sl * 1024:sl * 1024 + 1024],
                in_=dram[f"gatF{g}"][j * 128:j * 128 + 128, :]),
               wait=[(cCC, cc_f), (cDV, fs_used.get(j - 2, ph1_dv)),
                     (cPE, zpe[NZ - 1])], inc=qS)
            fld = qS.n
            for hl in range(16):
                h = hl0 + hl
                EM("vector", lambda e, j=j, sl=sl, hl=hl, h=h:
                   e.scalar_tensor_tensor(
                       out=bc_mr[:, hl * 64:hl * 64 + 64],
                       in0=xtmb[:, sl * 1024 + hl * 64:sl * 1024 + hl * 64 + 64],
                       scalar=WC[:, j * 128 + h:j * 128 + h + 1],
                       in1=bc_mr[:, hl * 64:hl * 64 + 64],
                       op0=ALU.mult, op1=ALU.add),
                   wait=[(qS, fld)], inc=cDV)
            fs_used[j] = cDV.n
        icomb_d = cDV.n
        if g == 0:
            tap_dump("ib0", bc_mr, 1024)
        eax = {}
        yomm = {}
        yoev = {}
        cedd = {}
        ea_used = {}
        for z in range(NZ):
            EM("sync", lambda e, g=g, z=z: e.dma_start(
                out=paf[0:1, 0:1024],
                in_=acum[g * 16:g * 16 + 8, z * 128:z * 128 + 128]),
               wait=[(cPE, ea_used.get(z - 1, zpe[NZ - 1]))], inc=qS)
            EM("sync", lambda e, g=g, z=z: e.dma_start(
                out=paf[64:65, 0:1024],
                in_=acum[g * 16 + 8:g * 16 + 16, z * 128:z * 128 + 128]),
               inc=qS)
            paf_d = qS.n
            corr_ds = {}
            def p2A(hl):
                li = z * 16 + hl
                h = hl0 + hl
                s2 = hl % 2
                ebk = sm32 if s2 == 0 else st0
                pr = (hl // 8) * 64
                EM("vector", lambda e, z=z, hl=hl, h=h: e.scalar_tensor_tensor(
                    out=arena[:, SH0 + (z * 16 + hl) * 64:SH0 + (z * 16 + hl) * 64 + 64],
                    in0=bc_mr[:, hl * 64:hl * 64 + 64],
                    scalar=PB[:, z * 128 + h:z * 128 + h + 1],
                    in1=arena[:, SH0 + (z * 16 + hl) * 64:SH0 + (z * 16 + hl) * 64 + 64],
                    op0=ALU.mult, op1=ALU.add),
                   wait=[(cDV, icomb_d)], inc=cDV)
                corr_ds[li] = cDV.n
                EM("tensor", lambda e, hl=hl, ebk=ebk, pr=pr: e.matmul(
                    ebk[:, 320:448],
                    ones128[pr:pr + 1, 0:128],
                    paf[pr:pr + 1, (hl % 8) * 128:(hl % 8) * 128 + 128],
                    start=True, stop=True),
                   wait=[(qS, paf_d), (cAC, eax.get(li - 2, 0)),
                         (cDV, yoev.get(li - 2, ph1_dv))], inc=cPE)
                ea_mm = cPE.n
                EM("scalar", lambda e, s2=s2, ebk=ebk: e.activation(
                    out=e1s[:, s2 * 128:s2 * 128 + 128],
                    in_=ebk[:, 320:448],
                    func=AF.Exp),
                   wait=[(cPE, ea_mm), (cPL, cedd.get(li - 2, 0))], inc=cAC)
                eax[li] = cAC.n
                EM("gpsimd", lambda e, z=z, s2=s2, s4=hl % 4: e.tensor_tensor(
                    out=ces[:, (hl % 4) * 128:(hl % 4) * 128 + 128],
                    in0=Cfm[:, z * 128:z * 128 + 128],
                    in1=e1s[:, s2 * 128:s2 * 128 + 128], op=ALU.mult),
                   wait=[(cAC, eax[li]), (cPE, yomm.get(li - 4, 0))], inc=cPL)
                cedd[li] = cPL.n

            def p2B(hl):
                li = z * 16 + hl
                s4 = hl % 4
                s2 = hl % 2
                pb = (hl % 2) * 64
                ybk = psY if s2 == 0 else psS
                EM("tensor", lambda e, z=z, hl=hl, s4=s4, pb=pb, ybk=ybk:
                   e.matmul(
                       ybk[pb:pb + 64, 320:448],
                       arena[:, SH0 + (z * 16 + hl) * 64:SH0 + (z * 16 + hl) * 64 + 64],
                       ces[:, s4 * 128:s4 * 128 + 128], start=True, stop=True),
                   wait=[(cPL, cedd[li]), (cDV, corr_ds[li]),
                         (cDV, yoev.get(li - 2, ph1_dv))], inc=cPE)
                yomm[li] = cPE.n
                EM("vector", lambda e, z=z, hl=hl, pb=pb, ybk=ybk: e.tensor_tensor(
                    out=arena[pb:pb + 64, HS0 + (hl // 2) * 1024 + z * 128:HS0 + (hl // 2) * 1024 + z * 128 + 128],
                    in0=arena[pb:pb + 64, HS0 + (hl // 2) * 1024 + z * 128:HS0 + (hl // 2) * 1024 + z * 128 + 128],
                    in1=ybk[pb:pb + 64, 320:448],
                    op=ALU.add),
                   wait=[(cPE, yomm[li])], inc=cDV)
                yoev[li] = cDV.n

            for i in range(16 + 2):
                if i < 16:
                    p2A(i)
                if i >= 2:
                    p2B(i - 2)
            ea_used[z] = cPE.n
        ph2_dv = cDV.n
        if g == 0:
            tap_dump("y0", arena[:, HS0:HS0 + 8192], 8192)
        if parts == 'ph2':
            barrier()
            return
        gat_d = {}
        gat_rd = {}
        sqg_mm = {}
        for hc in range(8):
            sl = hc % 2
            EM("sync", lambda e, g=g, hc=hc, sl=sl: e.dma_start(
                out=castbf[:, sl * 1024:sl * 1024 + 1024],
                in_=dram["gatepark"][g * 1024 + hc * 128:g * 1024 + hc * 128 + 128, :]),
               wait=[(qS, gp_st[hc]), (cPL, gat_rd.get(hc - 2, 0))], inc=qS)
            gld = qS.n
            EM("scalar", lambda e, sl=sl: e.activation(
                out=castbf[:, sl * 1024:sl * 1024 + 1024],
                in_=castbf[:, sl * 1024:sl * 1024 + 1024], func=AF.Silu),
               wait=[(qS, gld)], inc=cAC)
            sil_d = cAC.n
            gat_d[hc] = sil_d
            EM("gpsimd", lambda e, hc=hc, sl=sl: e.tensor_tensor(
                out=arena[:, HS0 + hc * 1024:HS0 + hc * 1024 + 1024],
                in0=arena[:, HS0 + hc * 1024:HS0 + hc * 1024 + 1024],
                in1=castbf[:, sl * 1024:sl * 1024 + 1024], op=ALU.mult),
               wait=[(cAC, sil_d), (cDV, ph2_dv)], inc=cPL)
            gtd = cPL.n
            gat_rd[hc] = gtd
            EM("scalar", lambda e, hc=hc, gtd=gtd: e.activation(
                out=sqbf[:, 0:1024], in_=arena[:, HS0 + hc * 1024:HS0 + hc * 1024 + 1024],
                func=AF.Square),
               wait=[(cPL, gtd), (cPE, sqg_mm.get(hc - 1, 0))], inc=cAC)
            sq_d = cAC.n
            for tt in range(2):
                stp = st0 if tt == 0 else st1
                EM("tensor", lambda e, hc=hc, tt=tt, stp=stp: e.matmul(
                    stp[0:1, 0:512], onescbf[:, 0:1],
                    sqbf[:, tt * 512:tt * 512 + 512],
                    start=(hc == 0), stop=(hc == 7)),
                   wait=[(cAC, sq_d)], inc=cPE)
            sqg_mm[hc] = cPE.n
        gst_pe = cPE.n
        for tt in range(2):
            stp = st0 if tt == 0 else st1
            EM("vector", lambda e, tt=tt, stp=stp: e.tensor_copy(
                out=srow4[0:1, tt * 512:tt * 512 + 512], in_=stp[0:1, 0:512]),
               wait=[(cPE, gst_pe)], inc=cDV)
        growd = cDV.n
        for tt in range(2):
            stp = st0 if tt == 0 else st1
            EM("tensor", lambda e, tt=tt, stp=stp: e.matmul(
                stp[:, 0:512], ones_f[0:1, 0:128],
                srow4[0:1, tt * 512:tt * 512 + 512], start=True, stop=True),
               wait=[(cDV, growd)], inc=cPE)
            EM("vector", lambda e, tt=tt, stp=stp: e.tensor_scalar(
                bc_r[:, tt * 512:tt * 512 + 512], stp[:, 0:512],
                1.0 / 1024, EPS, ALU.mult, ALU.add),
               wait=[(cPE, cPE.n)], inc=cDV)
        EM("scalar", lambda e: e.activation(out=bc_r[:, :], in_=bc_r[:, :],
                                            func=AF.Sqrt),
           wait=[(cDV, cDV.n)], inc=cAC)
        EM("vector", lambda e: e.reciprocal(bc_r[:, :], bc_r[:, :]),
           wait=[(cAC, cAC.n)], inc=cDV)
        rb_d = cDV.n
        gat_all = cPL.n
        for hc in range(8):
            for tt in range(2):
                EM("vector", lambda e, g=g, hc=hc, tt=tt:
                   e.scalar_tensor_tensor(
                       out=arena[:, HS0 + hc * 1024 + tt * 512:HS0 + hc * 1024 + tt * 512 + 512],
                       in0=arena[:, HS0 + hc * 1024 + tt * 512:HS0 + hc * 1024 + tt * 512 + 512],
                       scalar=nrmw[:, g * 8 + hc:g * 8 + hc + 1],
                       in1=bc_r[:, tt * 512:tt * 512 + 512],
                       op0=ALU.mult, op1=ALU.mult),
                   wait=[(cDV, rb_d), (cPL, gat_all)], inc=cDV)
        ynorm_d = cDV.n
        if g == 0:
            tap_dump("yn0", arena[:, HS0:HS0 + 8192], 8192)
        op_mm = {}
        op_ev = {}
        for ob in range(8):
            wb = wWA if ob % 2 == 0 else wWB
            EM("gpsimd", lambda e, g=g, ob=ob, wb=wb: e.dma_start(
                out=wb[:, :],
                in_=dram["Wout_t"][g, ob, :, :, :, :].rearrange("d k c o -> d (k c o)")),
               wait=[(cPE, op_mm.get(ob - 2, 0))], inc=qG)
            wld = qG.n
            for oci in range(4):
                oc = ob * 4 + oci
                for tt in range(2):
                    pp = [mmA, mmB, psY, psS][(oc % 2) * 2 + tt]
                    for kc in range(8):
                        EM("tensor", lambda e, oci=oci, tt=tt, kc=kc, pp=pp,
                           wb=wb: e.matmul(
                               pp[:, 0:512],
                               wb[:, (kc * 4 + oci) * 128:(kc * 4 + oci) * 128 + 128],
                               arena[:, HS0 + kc * 1024 + tt * 512:HS0 + kc * 1024 + tt * 512 + 512],
                               start=(kc == 0), stop=(kc == 7)),
                           wait=([(qG, wld), (cDV, ynorm_d),
                                  (cDV, op_ev.get((oc - 1, tt), 0))]
                                 if kc == 0 else []), inc=cPE)
                    mmdone = cPE.n
                    op_mm[ob] = mmdone
                    sl = tt
                    EM("vector", lambda e, sl=sl, pp=pp: e.tensor_copy(
                        out=xin32[:, sl * 512:sl * 512 + 512], in_=pp[:, 0:512]),
                       wait=[(cPE, mmdone), (qG, op_st.get((oc - 1, tt), 0))],
                       inc=cDV)
                    evd = cDV.n
                    op_ev[(oc, tt)] = evd
                    EM("gpsimd", lambda e, oc=oc, tt=tt, sl=sl: e.dma_start(
                        out=dram["park2"][oc * 128:oc * 128 + 128,
                                          tt * 512:tt * 512 + 512],
                        in_=xin32[:, sl * 512:sl * 512 + 512],
                        accum_op=ALU.add),
                       wait=[(cDV, evd)], inc=qG)
                    op_st[(oc, tt)] = qG.n
        barrier()

    op_st = {}

    def idbf_id(e):
        return id32s[:, :]

    ln_phase("xT")
    if isinstance(do_ffn1, str):
        ffn_phase(1, parts=do_ffn1)
    elif do_ffn1:
        ffn_phase(1)
    else:
        EM("sync", lambda e: e.dma_start(out=dram["park1"], in_=dram["xT"]),
           inc=qS)
        barrier()
    ln_phase("park1")
    tap_dump("xn", arena[:, 0:32768], 32768)
    if do_mamba:
        dt_phase()
        if do_mamba == "dt":
            pass
        elif isinstance(do_mamba, str):
            for g in range(n_groups):
                mamba_group(g, parts=do_mamba)
        else:
            for g in range(n_groups):
                mamba_group(g)
    else:
        EM("sync", lambda e: e.dma_start(out=dram["park2"], in_=dram["park1"]),
           inc=qS)
        barrier()
    ln_phase("park2")
    if do_ffn2:
        ffn_phase(2)
    else:
        for kc in range(KD):
            EM("sync", lambda e, kc=kc: e.dma_start(
                out=outT[kc * 128:kc * 128 + 128, :],
                in_=dram["park2"][kc * 128:kc * 128 + 128, :]), inc=qS)
        barrier()
    fin = tuple((c, c.n) for c in ALLC if c.n > 0)
    def run_fin(eng, fin=fin):
        for c, v in fin:
            eng.wait_ge(c.sem, v)
    pl.ops["sync"].append(run_fin)

    with ctx:
        with nc.Block() as block:
            @block.sync
            def _(eng):
                for f in pl.ops["sync"]:
                    f(eng)

            @block.gpsimd
            def _(eng):
                for f in pl.ops["gpsimd"]:
                    f(eng)

            @block.tensor
            def _(eng):
                for f in pl.ops["tensor"]:
                    f(eng)

            @block.vector
            def _(eng):
                for f in pl.ops["vector"]:
                    f(eng)

            @block.scalar
            def _(eng):
                for f in pl.ops["scalar"]:
                    f(eng)
    return nc


def _prep_shared(i):
    f32 = np.float32
    out = {}
    Wb = np.asarray(i["L0_ffn_W_base"], f32)
    out["Wb_t"] = np.ascontiguousarray(
        Wb.T.reshape(32, 128, 32, 128).transpose(2, 1, 0, 3))
    out["bbimg"] = np.ascontiguousarray(
        np.asarray(i["L0_ffn_b_base"], f32).reshape(32, 128).T)
    Aw = np.asarray(i["L0_ffn_A_w"], f32)
    AcT = Aw.reshape(128, 4096).T
    out["Acat_img"] = np.ascontiguousarray(
        AcT.reshape(32, 128, 128).transpose(1, 0, 2).reshape(128, 4096))
    Bw = np.asarray(i["L0_ffn_B_w"], f32)
    out["Bcat_img"] = np.ascontiguousarray(
        Bw.transpose(0, 2, 1).reshape(128, 4096))
    Wg9 = np.concatenate([np.asarray(i["L0_ffn_W_gate"], f32),
                          np.asarray(i["L0_ffn_W_thr"], f32)[None]], 0)
    out["Wg9_img"] = np.ascontiguousarray(
        Wg9.T.reshape(32, 128, 9).transpose(1, 0, 2).reshape(128, 288))
    out["bthr_col"] = np.full((128, 1), np.asarray(i["L0_ffn_b_thr"], f32)[0],
                              f32)
    Win = np.asarray(i["L0_attn_in_proj_w"], f32)
    wint = np.empty((G, 18, 128, 32, 128), f32)
    for g in range(G):
        rows = np.concatenate([
            np.arange(8192 + g * 1024, 8192 + (g + 1) * 1024),
            np.arange(16384 + g * 128, 16384 + (g + 1) * 128),
            np.arange(17408 + g * 128, 17408 + (g + 1) * 128),
        ])
        rows = np.concatenate([np.arange(g * 1024, (g + 1) * 1024), rows])
        WgT = Win[rows].T
        wint[g] = WgT.reshape(32, 128, 18, 128).transpose(2, 1, 0, 3)
    out["Win_t"] = wint
    Wdt = Win[18432:18560]
    out["Wdt_img"] = np.ascontiguousarray(
        Wdt.T.reshape(32, 128, 128).transpose(1, 0, 2).reshape(128, 4096))
    Wout = np.asarray(i["L0_attn_out_proj_w"], f32)
    WoT = Wout.T
    out["Wout_t"] = np.ascontiguousarray(
        WoT.reshape(G, 8, 128, 8, 4, 128).transpose(0, 3, 2, 1, 4, 5))
    convw = np.asarray(i["L0_attn_conv_w"], f32)[:, 0, :]
    convb = np.asarray(i["L0_attn_conv_b"], f32)
    cwi = np.empty((128, 320), f32)
    cbi = np.empty((128, 80), f32)
    for g in range(G):
        for c in range(10):
            if c < 8:
                ch = g * 1024 + c * 128 + np.arange(128)
            elif c == 8:
                ch = 8192 + g * 128 + np.arange(128)
            else:
                ch = 9216 + g * 128 + np.arange(128)
            cwi[:, (g * 10 + c) * 4:(g * 10 + c) * 4 + 4] = convw[ch]
            cbi[:, g * 10 + c] = convb[ch]
    out["convw_img"] = cwi
    out["convb_img"] = cbi
    out["dtb_col"] = np.asarray(i["L0_attn_dt_bias"], f32).reshape(128, 1)
    out["Alog_col"] = np.asarray(i["L0_attn_A_log"], f32).reshape(128, 1)
    out["D_col"] = np.asarray(i["L0_attn_D"], f32).reshape(128, 1)
    out["normw_img"] = np.ascontiguousarray(
        np.asarray(i["L0_attn_norm_w"], f32).reshape(64, 128).T)
    out["id32"] = np.eye(128, dtype=f32)
    out["triu"] = np.triu(np.ones((128, 128), f32))
    rep = np.zeros((8, 128), f32)
    for e in range(8):
        rep[e, e * 16:(e + 1) * 16] = 1.0
    out["repmat"] = rep
    return out


def _prep_core(i, k, shared):
    f32 = np.float32
    x = np.asarray(i["L0_attn_x"], f32)
    b, q = k // 4, k % 4
    d = dict(shared)
    d["xT"] = np.ascontiguousarray(x[b, q * T:(q + 1) * T, :].T)
    ph = np.zeros((128, 4), f32)
    if q != 0:
        ph[:, q - 1] = 1.0
    d["phalo"] = ph
    mc = np.zeros((128, 4), f32)
    mc[:, :q] = 1.0
    d["mcoef"] = mc
    cc = np.zeros((128, 16), f32)
    for j in range(4):
        for ii in range(4):
            if j < ii < q:
                cc[:, j * 4 + ii] = 1.0
    d["ccoef"] = cc
    return d


def kernel(**inputs):
    shared = _prep_shared(inputs)
    in_maps = [_prep_core(inputs, k, shared) for k in range(8)]
    nc = build_program()
    res = run_bass_kernel_spmd(nc, in_maps, core_ids=list(range(8)))
    x = np.asarray(inputs["L0_attn_x"])
    out = np.empty((2, 4096, 4096), np.float32)
    for k in range(8):
        b, q = k // 4, k % 4
        out[b, q * T:(q + 1) * T, :] = res.results[k]["outT"].T
    return out
